# version 05
# baseline (speedup 1.0000x reference)
import numpy as np
import concourse.bass as bass
import concourse.mybir as mybir
from concourse.bass_utils import run_bass_kernel_spmd

F32 = mybir.dt.float32
BF16 = mybir.dt.bfloat16
U8 = mybir.dt.uint8
AF = mybir.ActivationFunctionType
ALU = mybir.AluOpType

NCORE = 8
T, D, H, HD = 8192, 2048, 16, 128
CH = 256
NCH = T // CH
NSLOT = 4
TOK = NSLOT * CH
NM = D // 128
FF = 4096
NFO = FF // 128
LPRE = [16 * j + 14 for j in range(NSLOT)]
PMB = [0]
for _j in range(1, NSLOT):
    PMB.append(PMB[-1] + LPRE[_j - 1])
PMTOT = PMB[-1] + LPRE[-1]
EPS = 1e-5
NEG = -1e30
ASCALE = 1.0 / 128.0

ENGS = ("tensor", "scalar", "vector", "sync", "gpsimd")


def _owned_positions(c):
    return np.concatenate(
        [np.arange(CH * (8 * j + c), CH * (8 * j + c) + CH) for j in range(NSLOT)]
    )


class Builder:
    def __init__(self, stage=6, dbg=False):
        self.stage = stage
        self.dbg = dbg
        nc = self.nc = bass.Bass("TRN2", target_bir_lowering=False)
        self.q = {e: [] for e in ENGS}
        self._wait_hw = {}
        self.sem = {}
        self.cnt = {}
        for s in ("lds", "ldg", "mm", "sc", "ve", "cc", "out"):
            self.sem[s] = nc.alloc_semaphore(s)
            self.cnt[s] = 0
        self._declare_dram()
        self._declare_sbuf()

    def w(self, eng, sname, val=None):
        v = self.cnt[sname] if val is None else val
        if v is None or v <= 0:
            return
        hw = self._wait_hw.setdefault(eng, {})
        if hw.get(sname, -1) >= v:
            return
        hw[sname] = v
        h = self.sem[sname]
        self.q[eng].append(lambda e, h=h, v=v: e.wait_ge(h, v))

    def op(self, eng, f, inc=None, n=1):
        if inc is None:
            self.q[eng].append(f)
            return None
        self.cnt[inc] += n
        h = self.sem[inc]
        self.q[eng].append(lambda e, f=f, h=h, n=n: f(e).then_inc(h, n))
        return self.cnt[inc]

    def dma(self, eng, out, in_, sname):
        return self.op(
            eng, lambda e, o=out, i=in_: e.dma_start(out=o, in_=i), inc=sname, n=16
        )

    def mark(self, eng, inc):
        return self.op(eng, lambda e: e.nop(), inc=inc)

    def _declare_dram(self):
        nc = self.nc
        t = nc.dram_tensor
        self.d_x = t("x_fm", [D, TOK], F32, kind="ExternalInput").ap()
        self.d_wq = t("wq", [D, D], F32, kind="ExternalInput").ap()
        self.d_wk = t("wk", [D, D], F32, kind="ExternalInput").ap()
        self.d_wv = t("wv", [D, D], F32, kind="ExternalInput").ap()
        self.d_wo = t("wo_t", [NM, D, 128], F32, kind="ExternalInput").ap()
        self.d_w0a = t("w0a_t", [NFO, D, 128], F32, kind="ExternalInput").ap()
        self.d_w1a = t("w1a_t", [NFO, D, 128], F32, kind="ExternalInput").ap()
        self.d_w0b = t("w0b_t", [NM, FF, 128], F32, kind="ExternalInput").ap()
        self.d_w1b = t("w1b_t", [NM, FF, 128], F32, kind="ExternalInput").ap()
        self.d_b0a = t("b0a_t", [128, NFO], F32, kind="ExternalInput").ap()
        self.d_b1a = t("b1a_t", [128, NFO], F32, kind="ExternalInput").ap()
        self.d_b0b = t("b0b_t", [128, NM], F32, kind="ExternalInput").ap()
        self.d_b1b = t("b1b_t", [128, NM], F32, kind="ExternalInput").ap()
        self.d_wg = t("wg_t", [NM, 128, 2], F32, kind="ExternalInput").ap()
        self.d_tab = t("rope_tabs", [2, 128, TOK], F32, kind="ExternalInput").ap()
        self.d_pad = t("padmask", [128, PMTOT], F32, kind="ExternalInput").ap()
        self.d_out = t("y", [TOK, D], F32, kind="ExternalOutput").ap()
        self.cc_in = t("cc_in", [NSLOT, 2, H, 128 * CH], BF16)
        self.kv_all = t("kv_all", [NCORE, NSLOT, 2, H, 128 * CH], BF16)
        self.x_spill = t("x_spill", [NM, 128, TOK], F32).ap()
        import ml_dtypes as _mld
        dmask = np.zeros((2, 128, CH), dtype=np.float32)
        for a in range(2):
            for k in range(128):
                dmask[a, k, : min(CH, 128 * a + k)] = np.float32(NEG)
        self.d_dmask = nc.inline_tensor(dmask.astype(_mld.bfloat16), name="diag_masks").ap()
        self.d_ident = nc.inline_tensor(np.eye(128, dtype=np.float32),
                                        name="ident_f32").ap()
        self.dbg_out = {}
        if self.dbg:
            st = self.stage
            if st == 1:
                self.dbg_out["xln"] = t("dbg_xln", [NM, 128, TOK], F32,
                                        kind="ExternalOutput").ap()
            if st == 2:
                self.dbg_out["q"] = t("dbg_q", [H, 128, TOK], F32,
                                      kind="ExternalOutput").ap()
            if st == 3:
                self.dbg_out["kv"] = t("dbg_kv", [NCORE, NSLOT, 2, H, 128 * CH],
                                       BF16, kind="ExternalOutput").ap()
            if st == 4:
                self.dbg_out["ot"] = t("dbg_ot", [H, 128, TOK], F32,
                                       kind="ExternalOutput").ap()
                self.dbg_out["xchk"] = t("dbg_xchk", [NM, 128, TOK], F32,
                                         kind="ExternalOutput").ap()
            if st == 5:
                self.dbg_out["x1"] = t("dbg_x1", [NM, 128, TOK], F32,
                                       kind="ExternalOutput").ap()
            if st == 6:
                self.dbg_out["xln2"] = t("dbg_xln2", [NM, 128, TOK], F32,
                                         kind="ExternalOutput").ap()
                self.dbg_out["mu8"] = t("dbg_mu8", [128, TOK], F32,
                                        kind="ExternalOutput").ap()
                self.dbg_out["out0"] = t("dbg_out0", [NM, 128, TOK], F32,
                                         kind="ExternalOutput").ap()

    def _declare_sbuf(self):
        nc = self.nc
        A = nc.alloc_sbuf_tensor
        self.ar_x = A("ar_x", [128, NM * TOK], F32)
        self.ar_a = A("ar_a", [128, 16384], BF16)
        self.ar_q = A("ar_q", [128, 16384], BF16)
        self.ar_o = A("ar_o", [128, 16384], BF16)
        self.ar_w = A("ar_w", [128, 8192], BF16)
        self.ar_t = A("ar_t", [128, 2048], F32)
        self.sb_pad = A("padmask_sb", [128, PMTOT], F32)
        self.sb_b0a = A("b0a_sb", [128, NFO], F32)
        self.sb_b1a = A("b1a_sb", [128, NFO], F32)
        self.sb_b0b = A("b0b_sb", [128, NM], F32)
        self.sb_b1b = A("b1b_sb", [128, NM], F32)
        self.sb_wg = A("wg_sb", [128, 2 * NM], BF16)
        self.sb_er = A("e_ring", [128, 3 * 512], BF16)
        self.sb_kd = A("kd_ring", [128, 2 * CH], BF16)
        self.sb_vd = A("vd_ring", [128, 2 * CH], BF16)
        self.sb_kpub = A("kpub_ring", [128, 2 * TOK], BF16)
        self.sb_vpub = A("vpub_ring", [128, 2 * 512], BF16)
        self.sb_dm = A("dmask_sb", [128, 2 * CH], BF16)
        self.sb_id = A("ident_sb", [128, 128], F32)
        self.sb_ones = A("ones_col", [128, 1], F32)
        self.sb_onesb = A("ones_col_bf", [128, 1], BF16)
        self.sb_cvec = A("c_col", [128, 1], F32)
        self.sb_eps = A("eps_c", [1, 1], F32)
        self.sb_orow = A("ones_row", [1, 128], F32)
        self.sb_rb = A("rb_sb", [128, 256], F32)
        self.sb_row = A("row_sb", [1, 1280], F32)
        self.sb_r1 = self.sb_row[0:1, 1024:1280]
        self.sb_m1 = self.sb_row[0:1, 0:1024]
        self.sb_mu8 = A("mask_u8", [128, TOK], U8)
        self.ps = nc.alloc_psum_tensor("PS", [128, 4096], F32)
        self.ar_of = self.ar_o.bitcast(F32)
        self.ar_wf = self.ar_w.bitcast(F32)
        self.ar_xb = self.ar_x.bitcast(BF16)
        self.ar_tb = self.ar_t.bitcast(BF16)

    def bank(self, b, cols=512, part=128, off=0):
        return self.ps[0:part, b * 512 + off: b * 512 + off + cols]

    def x_ch(self, m):
        return self.ar_x[:, m * TOK:(m + 1) * TOK]

    def xln_ch(self, m):
        return self.ar_a[:, m * TOK:(m + 1) * TOK]

    def q_bf(self, h):
        return self.ar_q[:, h * TOK:(h + 1) * TOK]

    def o_t(self, h):
        return self.ar_o[:, h * TOK:(h + 1) * TOK]

    def kc(self, tile):
        return self.ar_a[:, tile * 128:(tile + 1) * 128]

    def vc(self, tile):
        return self.ar_a[:, 7936 + tile * 128: 7936 + (tile + 1) * 128]

    def h_ch(self, fo):
        return self.ar_xb[:, fo * TOK:(fo + 1) * TOK]

    def out0_ch(self, fo2):
        return self.ar_q[:, fo2 * TOK:(fo2 + 1) * TOK]

    def emit_all(self):
        self.phase0_loads()
        self.v_xln = self.emit_layernorm(self.ar_of, "lds", self.v_x_loaded)
        if self.stage >= 2:
            self.phase_qkv()
        if self.stage >= 3:
            self.phase_allgather()
        if self.stage >= 4:
            self.phase_attention()
        if self.stage >= 5:
            self.phase_outproj()
        if self.stage >= 6:
            self.phase_ffn()
        self.phase_debug_and_flush()

    def phase0_loads(self):
        p = self
        p.v_x_loaded = p.dma(
            "sync",
            p.ar_x[:].rearrange("pp (m t) -> pp m t", m=NM),
            p.d_x.rearrange("(m pp) t -> pp m t", pp=128),
            "lds")
        p.dma("sync", p.sb_pad[:], p.d_pad, "lds")
        p.dma("sync", p.ar_t[:].rearrange("pp (a t) -> pp a t", a=2),
              p.d_tab.rearrange("a pp t -> pp a t"), "lds")
        p.dma("sync", p.sb_dm[:].rearrange("pp (a t) -> pp a t", a=2),
              p.d_dmask.rearrange("a pp t -> pp a t"), "lds")
        p.dma("sync", p.sb_id[:], p.d_ident, "lds")
        p.dma("sync", p.sb_b0a[:], p.d_b0a, "lds")
        p.dma("sync", p.sb_b1a[:], p.d_b1a, "lds")
        p.dma("sync", p.sb_b0b[:], p.d_b0b, "lds")
        p.dma("sync", p.sb_b1b[:], p.d_b1b, "lds")
        p.dma("gpsimd", p.sb_wg[:].rearrange("pp (m e) -> pp m e", m=NM),
              p.d_wg.rearrange("m pp e -> pp m e"), "ldg")
        p.op("vector", lambda e: e.memset(p.sb_ones[:], 1.0), inc="ve")
        p.op("vector", lambda e: e.memset(p.sb_onesb[:], 1.0), inc="ve")
        p.op("vector", lambda e: e.memset(p.sb_cvec[:], 1.0 / D), inc="ve")
        p.op("vector", lambda e: e.memset(p.sb_eps[:], float(EPS)), inc="ve")
        p.v_consts = p.op("vector", lambda e: e.memset(p.sb_orow[:], 1.0),
                          inc="ve")

    def emit_layernorm(self, scr, wsem, wval):
        p, ps = self, self.ps
        mean_b = scr[:, 0:1024]
        rstd_b = scr[:, 1024:2048]
        tmp = scr[:, 2048:3072]
        tmp2 = [scr[:, 2048:3072], scr[:, 3072:4096]]
        p.w("tensor", wsem, wval)
        p.w("tensor", "ve", p.v_consts)
        vmm = None
        for m in range(NM):
            for hh in range(2):
                vmm = p.op(
                    "tensor",
                    lambda e, m=m, hh=hh: e.matmul(
                        ps[0:1, hh * 512:(hh + 1) * 512], p.sb_cvec[:],
                        p.x_ch(m)[:, hh * 512:(hh + 1) * 512],
                        start=(m == 0), stop=(m == NM - 1)),
                    inc="mm" if (m == NM - 1 and hh == 1) else None)
        p.w("scalar", wsem, wval)
        p.w("scalar", "ve", p.v_consts)
        p.w("scalar", "mm", vmm)
        v_m1 = p.op("scalar", lambda e: e.copy(p.sb_m1[:], ps[0:1, 0:1024]),
                    inc="sc")
        p.w("tensor", "sc", v_m1)
        vmb = None
        for hh in range(2):
            vmb = p.op("tensor",
                       lambda e, hh=hh: e.matmul(
                           ps[:, 1024 + hh * 512:1024 + (hh + 1) * 512],
                           p.sb_orow[:], p.sb_m1[0:1, hh * 512:(hh + 1) * 512],
                           start=True, stop=True),
                       inc="mm" if hh == 1 else None)
        p.w("scalar", "mm", vmb)
        v_mbe = p.op("scalar", lambda e: e.copy(mean_b, ps[:, 1024:2048]),
                     inc="sc")
        p.w("vector", "sc", v_mbe)
        var_mm = {}
        for m in range(NM):
            tb = tmp2[m % 2]
            if m >= 2:
                p.w("vector", "mm", var_mm[m - 2])
            vsq = p.op("vector",
                       lambda e, m=m, tb=tb: e.tensor_sub(tb, p.x_ch(m),
                                                          mean_b), inc="ve")
            vsq = p.op("vector", lambda e, tb=tb: e.tensor_mul(tb, tb, tb),
                       inc="ve")
            p.w("tensor", "ve", vsq)
            for hh in range(2):
                vmm = p.op(
                    "tensor",
                    lambda e, m=m, hh=hh, tb=tb: e.matmul(
                        ps[0:1, 2048 + hh * 512:2048 + (hh + 1) * 512],
                        p.sb_cvec[:], tb[:, hh * 512:(hh + 1) * 512],
                        start=(m == 0), stop=(m == NM - 1)),
                    inc="mm")
            var_mm[m] = vmm
        p.w("scalar", "mm", vmm)
        p.op("scalar", lambda e: e.activation(p.sb_m1[:], ps[0:1, 2048:3072],
                                              AF.Ln, bias=p.sb_eps[:]), inc="sc")
        v_r1 = p.op("scalar", lambda e: e.activation(p.sb_m1[:], p.sb_m1[:],
                                                     AF.Exp, scale=-0.5),
                    inc="sc")
        p.w("tensor", "sc", v_r1)
        vrb = None
        for hh in range(2):
            vrb = p.op("tensor",
                       lambda e, hh=hh: e.matmul(
                           ps[:, 3072 + hh * 512:3072 + (hh + 1) * 512],
                           p.sb_orow[:], p.sb_m1[0:1, hh * 512:(hh + 1) * 512],
                           start=True, stop=True),
                       inc="mm" if hh == 1 else None)
        p.w("scalar", "mm", vrb)
        v_rbe = p.op("scalar", lambda e: e.copy(rstd_b, ps[:, 3072:4096]),
                     inc="sc")
        p.w("vector", "sc", v_rbe)
        lastv = None
        for m in range(NM):
            tb = tmp2[m % 2]
            p.op("vector", lambda e, m=m, tb=tb: e.tensor_sub(
                tb, p.x_ch(m), mean_b), inc=None)
            lastv = p.op("vector",
                         lambda e, m=m, tb=tb: e.tensor_mul(
                             p.xln_ch(m), tb, rstd_b), inc="ve")
        return lastv

    def phase_qkv(self):
        p, ps = self, self.ps

        def wpair_slot(i):
            return p.ar_w[:, (i % 2) * 4096:(i % 2) * 4096 + 4096]

        def swp_slot(lh):
            return p.ar_of[:, 3072 + lh * 1024: 3072 + (lh + 1) * 1024]

        def t1_slot(lh):
            return p.ar_of[:, 5120 + lh * 1024: 5120 + (lh + 1) * 1024]

        C1 = p.ar_t[:, 0:1024]
        C2 = p.ar_t[:, 1024:2048]

        def load_wpair(pp, wait_mm):
            is_k = pp >= 8
            hpair = pp % 8
            src = (p.d_wk if is_k else p.d_wq).rearrange(
                "(m pp2) qq -> pp2 m qq", pp2=128)[
                :, :, hpair * 256:(hpair + 1) * 256]
            dst = wpair_slot(pp)[:].rearrange("pp2 (m c) -> pp2 m c", m=NM)
            if wait_mm is not None:
                p.w("gpsimd", "mm", wait_mm)
            return p.dma("gpsimd", dst, src, "ldg")

        wpair_loaded = [load_wpair(0, None), load_wpair(1, None)]
        pair_mm_done = []
        pair_ev_done = []
        k_pub_val = {}
        for pp in range(16):
            is_k = pp >= 8
            hpair = pp % 8
            g0 = (pp % 2) * 2
            p.w("tensor", "ldg", wpair_loaded[pp])
            if pp == 0:
                p.w("tensor", "ve", self.v_xln)
            if pp >= 2:
                p.w("tensor", "ve", pair_ev_done[pp - 2])
            vmm = None
            for m in range(NM):
                for lh in range(2):
                    for hh in range(2):
                        last = m == NM - 1 and lh == 1 and hh == 1
                        vmm = p.op(
                            "tensor",
                            lambda e, m=m, lh=lh, hh=hh, g0=g0, pp=pp: e.matmul(
                                ps[:, (g0 + lh) * 1024 + hh * 512:
                                   (g0 + lh) * 1024 + (hh + 1) * 512],
                                wpair_slot(pp)[:, m * 256 + lh * 128:
                                               m * 256 + (lh + 1) * 128],
                                p.xln_ch(m)[:, hh * 512:(hh + 1) * 512],
                                start=(m == 0), stop=(m == NM - 1)),
                            inc="mm" if last else None)
            pair_mm_done.append(vmm)
            if pp + 2 < 16:
                wpair_loaded.append(load_wpair(pp + 2, vmm))
            p.w("scalar", "mm", vmm)
            p.w("vector", "mm", vmm)
            if pp == 0:
                p.w("scalar", "ve", self.v_xln)
            else:
                p.w("scalar", "ve", pair_ev_done[pp - 1])
            last_ve = None
            for lh in range(2):
                head = 2 * hpair + lh
                pg = ps[:, (g0 + lh) * 1024:(g0 + lh + 1) * 1024]
                sw, t1 = swp_slot(lh), t1_slot(lh)
                p.op("scalar",
                     lambda e, pg=pg, sw=sw: e.copy(sw[0:64, :], pg[64:128, :]),
                     inc=None)
                v_sw = p.op("scalar",
                            lambda e, pg=pg, sw=sw: e.copy(sw[64:128, :],
                                                           pg[0:64, :]),
                            inc="sc")
                dst = (p.sb_kpub[:, (head % 2) * TOK:(head % 2 + 1) * TOK]
                       if is_k else p.q_bf(head))
                p.w("vector", "sc", v_sw)
                p.op("vector", lambda e, sw=sw: e.tensor_mul(sw, sw, C2),
                     inc=None)
                p.op("vector", lambda e, pg=pg, t1=t1: e.tensor_mul(t1, pg, C1),
                     inc=None)
                if is_k and head - 2 in k_pub_val:
                    p.w("vector", "ldg", k_pub_val[head - 2])
                last_ve = p.op("vector",
                               lambda e, dst=dst, t1=t1, sw=sw: e.tensor_add(
                                   dst, t1, sw), inc="ve")
                if is_k:
                    p.w("gpsimd", "ve", last_ve)
                    for j in range(NSLOT):
                        k_pub_val[head] = p.dma(
                            "gpsimd",
                            p.cc_in[j, 0, head].rearrange("(dd t) -> dd t",
                                                          dd=128),
                            dst[:, j * CH:(j + 1) * CH], "ldg")
            pair_ev_done.append(last_ve)
        self.v_rope_done = pair_ev_done[-1]

        def wv_slot(i):
            return p.ar_o[:, (i % 2) * 8192:(i % 2) * 8192 + 8192]

        def load_wv(dg, wait_mm):
            src = p.d_wv.rearrange("(m pp2) qq -> pp2 m qq", pp2=128)[
                :, :, dg * 512:(dg + 1) * 512]
            dst = wv_slot(dg)[:].rearrange("pp2 (m c) -> pp2 m c", m=NM)
            if dg == 0:
                p.w("gpsimd", "ve", self.v_rope_done)
            if wait_mm is not None:
                p.w("gpsimd", "mm", wait_mm)
            return p.dma("gpsimd", dst, src, "ldg")

        wv_loaded = [load_wv(0, None), load_wv(1, None)]
        vev = {}
        vpub_val = {}
        wv_group_done = []
        for dg in range(4):
            p.w("tensor", "ldg", wv_loaded[dg])
            if dg == 0:
                p.w("tensor", "ve", self.v_rope_done)
            vmm = None
            for qt in range(8):
                b = qt % 4
                if b in vev:
                    p.w("tensor", "sc", vev[b])
                for m in range(NM):
                    vmm = p.op(
                        "tensor",
                        lambda e, m=m, qt=qt, dg=dg, b=b: e.matmul(
                            p.bank(b),
                            p.xln_ch(m)[:, qt * 128:(qt + 1) * 128],
                            wv_slot(dg)[:, m * 512:(m + 1) * 512],
                            start=(m == 0), stop=(m == NM - 1)),
                        inc="mm" if m == NM - 1 else None)
                p.w("scalar", "mm", vmm)
                sl = qt % 2
                if sl in vpub_val:
                    p.w("scalar", "ldg", vpub_val[sl])
                vslot = p.sb_vpub[:, sl * 512: sl * 512 + 512]
                vev[b] = p.op("scalar",
                              lambda e, vslot=vslot, b=b: e.copy(vslot,
                                                                 p.bank(b)),
                              inc="sc")
                jj, half = qt // 2, qt % 2
                dstv = p.cc_in[jj, 1, 4 * dg:4 * dg + 4].rearrange(
                    "hz (th tl dd) -> tl hz th dd", th=2, dd=128)[:, :, half]
                p.w("gpsimd", "sc", vev[b])
                vpub_val[sl] = p.dma(
                    "gpsimd", dstv,
                    vslot.rearrange("pp2 (hz dd) -> pp2 hz dd", hz=4), "ldg")
            wv_group_done.append(vmm)
            if dg + 2 < 4:
                wv_loaded.append(load_wv(dg + 2, wv_group_done[dg]))
        self.v_wv_mm_last = wv_group_done[-1]

    def phase_allgather(self):
        p = self
        p.w("gpsimd", "ldg")
        self.v_cc = p.op(
            "gpsimd",
            lambda e: e.collective_compute(
                "AllGather", ALU.bypass,
                replica_groups=[list(range(NCORE))],
                ins=[p.cc_in.ap().opt()], outs=[p.kv_all.ap().opt()]),
            inc="cc")

    def phase_attention(self):
        p, ps = self, self.ps
        p.w("sync", "cc", self.v_cc)

        release = {}
        load_done = {}

        def emit_chunk_load(h, cchunk):
            r, jj = cchunk % NCORE, cchunk // NCORE
            if (h - 1, cchunk) in release:
                p.w("sync", "mm", release[(h - 1, cchunk)])
            p.dma("sync", p.ar_a[:, 2 * cchunk * 128: 2 * cchunk * 128 + 256],
                  p.kv_all[r, jj, 0, h].rearrange("(dd t) -> dd t", dd=128),
                  "lds")
            lv = p.dma(
                "sync",
                p.ar_a[:, 7936 + 2 * cchunk * 128:
                       7936 + 2 * cchunk * 128 + 256].rearrange(
                    "pp2 (th dd) -> pp2 th dd", th=2),
                p.kv_all[r, jj, 1, h].rearrange("(th tl dd) -> tl th dd",
                                                th=2, dd=128),
                "lds")
            load_done[(h, cchunk)] = lv

        def emit_diag_load(h, j):
            par = j % 2
            key = (h, "d", par) if (h, "d", par) in release else (
                (h - 1, "d", par) if (h - 1, "d", par) in release else None)
            if key is not None:
                p.w("sync", "mm", release[key])
            p.dma("sync", p.sb_kd[:, par * CH:(par + 1) * CH],
                  p.cc_in[j, 0, h].rearrange("(dd t) -> dd t", dd=128), "lds")
            lv = p.dma(
                "sync",
                p.sb_vd[:, par * CH:(par + 1) * CH].rearrange(
                    "pp2 (th dd) -> pp2 th dd", th=2),
                p.cc_in[j, 1, h].rearrange("(th tl dd) -> tl th dd",
                                           th=2, dd=128),
                "lds")
            load_done[(h, "d", j)] = lv

        er_owner = [None, None, None]
        st_owner = [None, None, None]
        pv_owner = [None, None]
        on_owner = [None, None]
        rb_owner = None
        last_vnorm = None
        gidx = 0
        pending_tail = []

        for h in range(H):
            for cchunk in range(NCH - 2, -1, -1):
                emit_chunk_load(h, cchunk)
            for j in range(NSLOT - 1, -1, -1):
                par = j % 2
                emit_diag_load(h, j)
                qap = p.q_bf(h)[:, j * 256:(j + 1) * 256]
                groups = [("pre", i) for i in range(LPRE[j] // 2 - 1, -1, -1)]
                groups.append(("diag", j))
                pv_bank, on_bank = 4 + par, 6 + par
                if pv_owner[par] is not None:
                    p.w("tensor", "ve", pv_owner[par])
                if on_owner[par] is not None:
                    p.w("tensor", "ve", on_owner[par])
                def emit_pv(pg):
                    (eslot_, v0_, v1_, first_, last_, vexp_, kind_, idx_,
                     er__) = pg
                    p.w("tensor", "sc", vexp_)
                    p.op("tensor",
                         lambda e, v0=v0_, eslot=eslot_, pv_bank=pv_bank,
                         first=first_: e.matmul(
                             p.bank(pv_bank, cols=256), v0, eslot[:, 0:256],
                             start=first, stop=False), inc=None)
                    p.op("tensor",
                         lambda e, v1=v1_, eslot=eslot_, pv_bank=pv_bank,
                         last=last_: e.matmul(
                             p.bank(pv_bank, cols=256), v1, eslot[:, 256:512],
                             start=False, stop=last), inc=None)
                    vpv = p.op("tensor",
                               lambda e, eslot=eslot_, on_bank=on_bank,
                               first=first_, last=last_: e.matmul(
                                   p.bank(on_bank, cols=512, part=1),
                                   p.sb_onesb[:], eslot[:, 0:512],
                                   start=first, stop=last), inc="mm")
                    er_owner[er__] = vpv
                    if kind_ == "pre":
                        release[(h, idx_)] = vpv
                    else:
                        release[(h, "d", par)] = vpv
                    return vpv

                pending = []
                for gi, (kind, idx) in enumerate(groups):
                    sb_ = gidx % 3
                    er_ = gidx % 3
                    if st_owner[sb_] is not None:
                        p.w("tensor", "sc", st_owner[sb_])
                    if kind == "pre":
                        p.w("tensor", "lds", load_done[(h, idx)])
                        k0, k1 = p.kc(2 * idx), p.kc(2 * idx + 1)
                        v0, v1 = p.vc(2 * idx), p.vc(2 * idx + 1)
                        bias = p.sb_pad[:, PMB[j] + 2 * idx:
                                        PMB[j] + 2 * idx + 1]
                    else:
                        p.w("tensor", "lds", load_done[(h, "d", idx)])
                        k0 = p.sb_kd[:, par * CH: par * CH + 128]
                        k1 = p.sb_kd[:, par * CH + 128: par * CH + 256]
                        v0 = p.sb_vd[:, par * CH: par * CH + 128]
                        v1 = p.sb_vd[:, par * CH + 128: par * CH + 256]
                        bias = None
                    p.op("tensor",
                         lambda e, k0=k0, sb_=sb_, qap=qap: e.matmul(
                             ps[:, sb_ * 512: sb_ * 512 + 256], k0, qap,
                             start=True, stop=True), inc=None)
                    vst = p.op("tensor",
                               lambda e, k1=k1, sb_=sb_, qap=qap: e.matmul(
                                   ps[:, sb_ * 512 + 256: sb_ * 512 + 512],
                                   k1, qap, start=True, stop=True), inc="mm")
                    if kind == "diag":
                        p.w("vector", "mm", vst)
                        p.op("vector",
                             lambda e, sb_=sb_: e.tensor_add(
                                 ps[:, sb_ * 512: sb_ * 512 + 256],
                                 ps[:, sb_ * 512: sb_ * 512 + 256],
                                 p.sb_dm[:, 0:256]), inc=None)
                        vdm = p.op("vector",
                                   lambda e, sb_=sb_: e.tensor_add(
                                       ps[:, sb_ * 512 + 256: sb_ * 512 + 512],
                                       ps[:, sb_ * 512 + 256: sb_ * 512 + 512],
                                       p.sb_dm[:, 256:512]), inc="ve")
                        p.w("scalar", "ve", vdm)
                    p.w("scalar", "mm", vst)
                    if er_owner[er_] is not None:
                        p.w("scalar", "mm", er_owner[er_])
                    eslot = p.sb_er[:, er_ * 512:(er_ + 1) * 512]
                    kw = {} if bias is None else {"bias": bias}
                    vexp = p.op("scalar",
                                lambda e, eslot=eslot, sb_=sb_, kw=kw:
                                e.activation(eslot,
                                             ps[:, sb_ * 512:(sb_ + 1) * 512],
                                             AF.Exp, scale=float(ASCALE), **kw),
                                inc="sc")
                    st_owner[sb_] = vexp
                    if pending_tail:
                        pending_tail.pop(0)()
                    first, last = gi == 0, gi == len(groups) - 1
                    pending.append((eslot, v0, v1, first, last, vexp, kind,
                                    idx, er_))
                    if len(pending) > 2:
                        emit_pv(pending.pop(0))
                    gidx += 1
                while pending:
                    emit_pv(pending.pop(0))
                v_ones_done = p.cnt["mm"]

                def make_tail(h=h, j=j, par=par, pv_bank=pv_bank,
                              on_bank=on_bank, v_ones_done=v_ones_done):
                    def tail():
                        nonlocal rb_owner, last_vnorm
                        p.w("vector", "mm", v_ones_done)
                        row = p.sb_row
                        p.op("vector",
                             lambda e: e.tensor_copy(
                                 row[0:1, 0:512],
                                 p.bank(on_bank, cols=512, part=1)), inc=None)
                        p.op("vector",
                             lambda e: e.tensor_add(
                                 row[0:1, 512:768], row[0:1, 0:256],
                                 row[0:1, 256:512]), inc=None)
                        vrec = p.op("vector",
                                    lambda e: e.reciprocal(
                                        p.sb_r1[:], row[0:1, 512:768]),
                                    inc="ve")
                        on_owner[par] = vrec
                        p.w("tensor", "ve", vrec)
                        if rb_owner is not None:
                            p.w("tensor", "sc", rb_owner)
                        vrb = p.op("tensor",
                                   lambda e: e.matmul(p.bank(3, cols=256),
                                                      p.sb_orow[:], p.sb_r1[:],
                                                      start=True, stop=True),
                                   inc="mm")
                        p.w("scalar", "mm", vrb)
                        if last_vnorm is not None:
                            p.w("scalar", "ve", last_vnorm)
                        rb_owner = p.op("scalar",
                                        lambda e: e.copy(p.sb_rb[:],
                                                         p.bank(3, cols=256)),
                                        inc="sc")
                        p.w("vector", "sc", rb_owner)
                        last_vnorm = p.op(
                            "vector",
                            lambda e: e.tensor_mul(
                                p.o_t(h)[:, j * 256:(j + 1) * 256],
                                p.bank(pv_bank, cols=256), p.sb_rb[:]),
                            inc="ve")
                        pv_owner[par] = last_vnorm
                    return tail

                pending_tail.append(make_tail())
        while pending_tail:
            pending_tail.pop(0)()
        self.v_attn = p.cnt["ve"]

    def phase_outproj(self):
        p, ps = self, self.ps

        def wo_slot(i):
            return p.ar_tb[:, (i % 2) * 2048:(i % 2) * 2048 + 2048]

        def load_wo(m, wait_mm):
            src = p.d_wo[m].rearrange("(hi pp2) c -> pp2 hi c", pp2=128)
            dst = wo_slot(m)[:].rearrange("pp2 (hi c) -> pp2 hi c", hi=NM)
            if m == 0:
                p.w("gpsimd", "ve", self.v_rope_done)
            if wait_mm is not None:
                p.w("gpsimd", "mm", wait_mm)
            return p.dma("gpsimd", dst, src, "ldg")

        wo_loaded = [load_wo(0, None), load_wo(1, None)]
        wo_used = []
        v_res = [None, None]
        for m in range(NM):
            g0 = (m % 2) * 2
            p.w("tensor", "ldg", wo_loaded[m])
            if m == 0:
                p.w("tensor", "ve", self.v_attn)
            if v_res[m % 2] is not None:
                p.w("tensor", "ve", v_res[m % 2])
            vmm = None
            for hd in range(NM):
                for hh in range(2):
                    vmm = p.op(
                        "tensor",
                        lambda e, m=m, hd=hd, hh=hh, g0=g0: e.matmul(
                            ps[:, g0 * 1024 + hh * 512:
                               g0 * 1024 + (hh + 1) * 512],
                            wo_slot(m)[:, hd * 128:(hd + 1) * 128],
                            p.o_t(hd)[:, hh * 512:(hh + 1) * 512],
                            start=(hd == 0), stop=(hd == NM - 1)),
                        inc="mm" if (hd == NM - 1 and hh == 1) else None)
            wo_used.append(vmm)
            if m + 2 < NM:
                wo_loaded.append(load_wo(m + 2, vmm))
            p.w("vector", "mm", vmm)
            v_res[m % 2] = p.op(
                "vector",
                lambda e, m=m, g0=g0: e.tensor_add(
                    p.x_ch(m), p.x_ch(m), ps[:, g0 * 1024:(g0 + 1) * 1024]),
                inc="ve")
        self.v_outproj_mm = wo_used[-1]
        self.v_x1 = p.cnt["ve"]

    def phase_ffn(self):
        p, ps = self, self.ps
        v_xln2 = self.emit_layernorm(self.ar_wf, "ve", self.v_x1)
        self.v_xln2_dbg = v_xln2
        v_ln2_sc = p.cnt["sc"]
        self._ffn_body(v_xln2, v_ln2_sc)

    def _ffn_body(self, v_xln2, v_ln2_sc):
        p, ps = self, self.ps
        import os
        _upto = os.environ.get("FFN_UPTO", "full")
        if _upto == "ln2":
            return
        p.w("tensor", "ve", v_xln2)
        vz = None
        for eidx in range(2):
            base = 2048 + eidx * 1024
            for m in range(NM):
                for hh in range(2):
                    vz = p.op(
                        "tensor",
                        lambda e, eidx=eidx, m=m, hh=hh, base=base: e.matmul(
                            ps[0:1, base + hh * 512: base + (hh + 1) * 512],
                            p.sb_wg[:, 2 * m + eidx: 2 * m + eidx + 1],
                            p.xln_ch(m)[:, hh * 512:(hh + 1) * 512],
                            start=(m == 0), stop=(m == NM - 1)),
                        inc="mm" if (m == NM - 1 and hh == 1) else None)
        p.w("scalar", "mm", vz)
        p.w("scalar", "ve", v_xln2)
        zrow = self.ar_wf
        p.op("scalar", lambda e: e.copy(zrow[0:1, 0:1024],
                                        ps[0:1, 2048:3072]), inc="sc")
        v_zev = p.op("scalar", lambda e: e.copy(zrow[0:1, 1024:2048],
                                                ps[0:1, 3072:4096]), inc="sc")
        p.w("vector", "sc", v_zev)
        v_ge = p.op("vector",
                    lambda e: e.tensor_tensor(zrow[0:1, 0:1024],
                                              zrow[0:1, 0:1024],
                                              zrow[0:1, 1024:2048],
                                              ALU.is_ge), inc="ve")
        p.w("tensor", "ve", v_ge)
        vmb = None
        for hh in range(2):
            vmb = p.op("tensor",
                       lambda e, hh=hh: e.matmul(
                           ps[:, 2048 + hh * 512:2048 + (hh + 1) * 512],
                           p.sb_orow[:], zrow[0:1, hh * 512:(hh + 1) * 512],
                           start=True, stop=True),
                       inc="mm" if hh == 1 else None)
        p.w("scalar", "mm", vmb)
        v_mu8 = p.op("scalar", lambda e: e.copy(p.sb_mu8[:], ps[:, 2048:3072]),
                     inc="sc")

        if _upto == "gate":
            return
        p.w("sync", "ve", v_xln2)
        v_spill = p.dma("sync",
                        p.x_spill.rearrange("m pp2 t -> pp2 m t"),
                        p.ar_x[:].rearrange("pp2 (m t) -> pp2 m t", m=NM),
                        "lds")

        def f1_slot(i):
            return p.ar_o[:, (i % 4) * 2048:(i % 4) * 2048 + 2048]

        def f2_slot(i):
            return p.ar_o[:, 8192 + (i % 2) * 4096:
                           8192 + (i % 2) * 4096 + 4096]

        f1_loaded, f2_loaded = [], []

        def load_f1(i, wait_mm):
            e_, fo = divmod(i, NFO)
            src = (p.d_w0a if e_ == 0 else p.d_w1a)[fo].rearrange(
                "(hi pp2) c -> pp2 hi c", pp2=128)
            dst = f1_slot(i)[:].rearrange("pp2 (hi c) -> pp2 hi c", hi=NM)
            if i == 0:
                p.w("gpsimd", "mm", self.v_outproj_mm)
            if wait_mm is not None:
                p.w("gpsimd", "mm", wait_mm)
            f1_loaded.append(p.dma("gpsimd", dst, src, "ldg"))

        def load_f2(i, wait_mm):
            e_, fo2 = divmod(i, NM)
            src = (p.d_w0b if e_ == 0 else p.d_w1b)[fo2].rearrange(
                "(hi pp2) c -> pp2 hi c", pp2=128)
            dst = f2_slot(i)[:].rearrange("pp2 (hi c) -> pp2 hi c", hi=NFO)
            if wait_mm is not None:
                p.w("gpsimd", "mm", wait_mm)
            f2_loaded.append(p.dma("gpsimd", dst, src, "ldg"))

        for i in range(4):
            load_f1(i, None)

        x_rb = lambda i: p.ar_wf[:, (i % 2) * 1024:(i % 2) * 1024 + 1024]
        x2_sl = lambda i: p.ar_t[:, (i % 2) * 1024:(i % 2) * 1024 + 1024]
        xT_sl = lambda i: p.ar_wf[:, 2048 + (i % 2) * 128:
                                  2048 + (i % 2) * 128 + 128]
        out1_sl = p.ar_w[:, 5120:6144]
        sel_sl = p.ar_w[:, 6144:7168]

        h_evac = {}
        f1_owner = {}
        f2_owner = {}
        v_out0 = {}
        v_f2mm = {}
        tr_owner = [None, None]
        xT_owner = [None, None]
        x2_owner = [None, None]
        x_rb_done = {}
        last_vsel = None
        last_vx2 = None

        for e_ in range(2):
            for fo in range(NFO):
                i = e_ * NFO + fo
                par = fo % 2
                g0 = par
                p.w("tensor", "ldg", f1_loaded[i])
                if i == 0:
                    p.w("tensor", "sc", v_ln2_sc)
                    p.w("tensor", "ve", v_xln2)
                if par in f1_owner:
                    p.w("tensor", f1_owner[par][0], f1_owner[par][1])
                vmm = None
                for fi in range(NM):
                    for hh in range(2):
                        vmm = p.op(
                            "tensor",
                            lambda e, i=i, fi=fi, hh=hh, g0=g0: e.matmul(
                                ps[:, g0 * 1024 + hh * 512:
                                   g0 * 1024 + (hh + 1) * 512],
                                f1_slot(i)[:, fi * 128:(fi + 1) * 128],
                                p.xln_ch(fi)[:, hh * 512:(hh + 1) * 512],
                                start=(fi == 0), stop=(fi == NM - 1)),
                            inc="mm" if (fi == NM - 1 and hh == 1) else None)
                if i + 4 < 2 * NFO:
                    load_f1(i + 4, vmm)
                if e_ == 0 and fo >= NFO - 2:
                    load_f2(fo - (NFO - 2), vmm)
                elif e_ == 1 and fo >= NFO - 2:
                    load_f2(NM + fo - (NFO - 2),
                            v_f2mm[(0, NM - 2 + (fo - (NFO - 2)))])
                p.w("scalar", "mm", vmm)
                if i == 0:
                    p.w("scalar", "lds", v_spill)
                if e_ == 1:
                    p.w("scalar", "mm", self.v_ffn2_e0_mm)
                bias = (p.sb_b0a if e_ == 0 else p.sb_b1a)[:, fo:fo + 1]
                func = AF.Gelu if e_ == 0 else AF.Relu
                h_evac[fo] = p.op(
                    "scalar",
                    lambda e, fo=fo, g0=g0, bias=bias, func=func: e.activation(
                        p.h_ch(fo), ps[:, g0 * 1024:(g0 + 1) * 1024], func,
                        bias=bias), inc="sc")
                f1_owner[par] = ("sc", h_evac[fo])

            for fo2 in range(NM):
                i = e_ * NM + fo2
                par = fo2 % 2
                g0 = 2 + par
                p.w("tensor", "ldg", f2_loaded[i])
                if e_ == 0 and fo2 in (0, 1):
                    p.w("tensor", "sc", v_mu8)
                if par in f2_owner:
                    p.w("tensor", f2_owner[par][0], f2_owner[par][1])
                vmm = None
                for fi2 in range(NFO):
                    if fi2 % 8 == 0:
                        p.w("tensor", "sc", h_evac[min(fi2 + 7, NFO - 1)])
                    for hh in range(2):
                        vmm = p.op(
                            "tensor",
                            lambda e, i=i, fi2=fi2, hh=hh, g0=g0: e.matmul(
                                ps[:, g0 * 1024 + hh * 512:
                                   g0 * 1024 + (hh + 1) * 512],
                                f2_slot(i)[:, fi2 * 128:(fi2 + 1) * 128],
                                p.h_ch(fi2)[:, hh * 512:(hh + 1) * 512],
                                start=(fi2 == 0), stop=(fi2 == NFO - 1)),
                            inc="mm" if (fi2 == NFO - 1 and hh == 1) else None)
                v_f2mm[(e_, fo2)] = vmm
                if e_ == 0 and fo2 + 2 < NM:
                    load_f2(fo2 + 2, vmm)
                elif e_ == 0 and fo2 >= NM - 2:
                    pass
                elif e_ == 1 and fo2 + 2 < NM:
                    load_f2(NM + fo2 + 2, vmm)
                if e_ == 0:
                    p.w("scalar", "mm", vmm)
                    v_out0[fo2] = p.op(
                        "scalar",
                        lambda e, fo2=fo2, g0=g0: e.activation(
                            p.out0_ch(fo2), ps[:, g0 * 1024:(g0 + 1) * 1024],
                            AF.Identity, bias=p.sb_b0b[:, fo2:fo2 + 1]),
                        inc="sc")
                    f2_owner[par] = ("sc", v_out0[fo2])
                else:
                    if fo2 < 2:
                        k = fo2
                        p.w("sync", "ve", v_xln2)
                        p.w("sync", "sc", v_mu8)
                        x_rb_done[k] = p.dma("sync", x_rb(k), p.x_spill[k],
                                             "lds")
                    p.w("scalar", "mm", vmm)
                    if last_vsel is not None:
                        p.w("scalar", "ve", last_vsel)
                    v_o1 = p.op(
                        "scalar",
                        lambda e, fo2=fo2, g0=g0: e.activation(
                            out1_sl, ps[:, g0 * 1024:(g0 + 1) * 1024],
                            AF.Identity, bias=p.sb_b1b[:, fo2:fo2 + 1]),
                        inc="sc")
                    p.w("vector", "sc", v_o1)
                    if fo2 == 0:
                        p.w("vector", "sc", v_mu8)
                    p.w("vector", "sc", v_out0.get(fo2))
                    last_vsel = p.op("vector",
                                     lambda e, fo2=fo2: e.select(
                                         sel_sl, p.sb_mu8[:], p.out0_ch(fo2),
                                         out1_sl), inc="ve")
                    p.w("vector", "lds", x_rb_done[fo2])
                    vx2 = p.op("vector",
                               lambda e, fo2=fo2: e.tensor_add(
                                   x2_sl(fo2), x_rb(fo2), sel_sl), inc="ve")
                    last_vx2 = vx2
                    f2_owner[par] = ("ve", vx2)
                    x2_owner[par] = vx2
                    if fo2 + 2 < NM:
                        k = fo2 + 2
                        if x2_owner[k % 2] is not None:
                            p.w("sync", "ve", x2_owner[k % 2])
                        x_rb_done[k] = p.dma("sync", x_rb(k), p.x_spill[k],
                                             "lds")
                    p.w("tensor", "ve", vx2)
                    for tt in range(8):
                        tpar = tt % 2
                        if tr_owner[tpar] is not None:
                            p.w("tensor", "sc", tr_owner[tpar])
                        vtr = p.op(
                            "tensor",
                            lambda e, fo2=fo2, tt=tt, tpar=tpar: e.transpose(
                                p.bank(tpar, cols=128),
                                x2_sl(fo2)[:, tt * 128:(tt + 1) * 128],
                                p.sb_id[:]), inc="mm")
                        p.w("scalar", "mm", vtr)
                        if xT_owner[tpar] is not None:
                            p.w("scalar", "out", xT_owner[tpar])
                        vev = p.op("scalar",
                                   lambda e, tpar=tpar: e.copy(
                                       xT_sl(tpar), p.bank(tpar, cols=128)),
                                   inc="sc")
                        tr_owner[tpar] = vev
                        p.w("sync", "sc", vev)
                        xT_owner[tpar] = p.dma(
                            "sync",
                            p.d_out[tt * 128:(tt + 1) * 128,
                                    fo2 * 128:(fo2 + 1) * 128],
                            xT_sl(tpar), "out")
            if e_ == 0:
                self.v_ffn2_e0_mm = v_f2mm[(0, NM - 1)]

    def phase_debug_and_flush(self):
        p = self
        d = self.dbg_out
        if "xln" in d:
            p.w("gpsimd", "ve", self.v_xln)
            p.dma("gpsimd", d["xln"].rearrange("m pp2 t -> pp2 m t"),
                  p.ar_a[:].rearrange("pp2 (m t) -> pp2 m t", m=NM), "ldg")
        if "q" in d:
            p.w("gpsimd", "ve", self.v_rope_done)
            p.w("gpsimd", "mm", self.v_wv_mm_last)
            for h in range(H):
                p.dma("gpsimd", d["q"][h], p.q_bf(h), "ldg")
        if "kv" in d:
            p.w("sync", "cc", self.v_cc)
            p.dma("sync", d["kv"], p.kv_all.ap(), "lds")
        if "ot" in d:
            p.w("gpsimd", "ve", self.v_attn)
            for h in range(H):
                p.dma("gpsimd", d["ot"][h], p.o_t(h), "ldg")
        if "xchk" in d:
            p.w("sync", "ve", self.v_attn)
            p.dma("sync", d["xchk"].rearrange("m pp2 t -> pp2 m t"),
                  p.ar_x[:].rearrange("pp2 (m t) -> pp2 m t", m=NM), "lds")
        if "x1" in d:
            p.w("sync", "ve", self.v_x1)
            p.dma("sync", d["x1"].rearrange("m pp2 t -> pp2 m t"),
                  p.ar_x[:].rearrange("pp2 (m t) -> pp2 m t", m=NM), "lds")
        if "xln2" in d:
            p.w("gpsimd", "ve", self.v_xln2_dbg)
            p.dma("gpsimd", d["xln2"].rearrange("m pp2 t -> pp2 m t"),
                  p.ar_a[:].rearrange("pp2 (m t) -> pp2 m t", m=NM), "ldg")
            p.w("gpsimd", "out")
            p.dma("gpsimd", d["mu8"], p.sb_mu8[:], "ldg")
            p.dma("gpsimd", d["out0"].rearrange("m pp2 t -> pp2 m t"),
                  p.ar_q[:].rearrange("pp2 (m t) -> pp2 m t", m=NM), "ldg")
        p.w("sync", "lds")
        p.w("sync", "out")
        p.w("gpsimd", "ldg")

    def finish(self):
        nc = self.nc
        qs = self.q
        with nc.Block() as block:
            @block.tensor
            def _(e):
                for f in qs["tensor"]:
                    f(e)

            @block.scalar
            def _(e):
                for f in qs["scalar"]:
                    f(e)

            @block.vector
            def _(e):
                for f in qs["vector"]:
                    f(e)

            @block.sync
            def _(e):
                for f in qs["sync"]:
                    f(e)

            @block.gpsimd
            def _(e):
                for f in qs["gpsimd"]:
                    f(e)
        return nc


def build_nc(stage=6, dbg=False):
    b = Builder(stage=stage, dbg=dbg)
    b.emit_all()
    return b.finish()



def _host_inputs(inputs):
    x = np.asarray(inputs["L0_attn_x"], dtype=np.float32)[0]
    wq = np.ascontiguousarray(
        np.asarray(inputs["L0_attn_wq"], np.float32).reshape(D, D))
    wk = np.ascontiguousarray(
        np.asarray(inputs["L0_attn_wk"], np.float32).reshape(D, D))
    wv = np.ascontiguousarray(
        np.asarray(inputs["L0_attn_wv"], np.float32).reshape(D, D))
    wo = np.asarray(inputs["L0_attn_wo"], np.float32).reshape(D, D)
    wo_t = np.ascontiguousarray(wo.reshape(D, NM, 128).transpose(1, 0, 2))
    W0a = np.asarray(inputs["L0_ffn_W0a"], np.float32)
    W1a = np.asarray(inputs["L0_ffn_W1a"], np.float32)
    W0b = np.asarray(inputs["L0_ffn_W0b"], np.float32)
    W1b = np.asarray(inputs["L0_ffn_W1b"], np.float32)
    w0a_t = np.ascontiguousarray(W0a.T.reshape(D, NFO, 128).transpose(1, 0, 2))
    w1a_t = np.ascontiguousarray(W1a.T.reshape(D, NFO, 128).transpose(1, 0, 2))
    w0b_t = np.ascontiguousarray(W0b.T.reshape(FF, NM, 128).transpose(1, 0, 2))
    w1b_t = np.ascontiguousarray(W1b.T.reshape(FF, NM, 128).transpose(1, 0, 2))
    b0a_t = np.ascontiguousarray(
        np.asarray(inputs["L0_ffn_b0a"], np.float32).reshape(NFO, 128).T)
    b1a_t = np.ascontiguousarray(
        np.asarray(inputs["L0_ffn_b1a"], np.float32).reshape(NFO, 128).T)
    b0b_t = np.ascontiguousarray(
        np.asarray(inputs["L0_ffn_b0b"], np.float32).reshape(NM, 128).T)
    b1b_t = np.ascontiguousarray(
        np.asarray(inputs["L0_ffn_b1b"], np.float32).reshape(NM, 128).T)
    Wg = np.asarray(inputs["L0_ffn_Wg"], np.float32)
    wg_t = np.ascontiguousarray(Wg.T.reshape(NM, 128, 2))

    shared = dict(wq=wq, wk=wk, wv=wv, wo_t=wo_t, w0a_t=w0a_t, w1a_t=w1a_t,
                  w0b_t=w0b_t, w1b_t=w1b_t, b0a_t=b0a_t, b1a_t=b1a_t,
                  b0b_t=b0b_t, b1b_t=b1b_t, wg_t=wg_t)

    half = HD // 2
    freqs = np.power(10000.0, -np.arange(half, dtype=np.float64) / half)
    in_maps = []
    for c in range(NCORE):
        pos = _owned_positions(c).astype(np.float64)
        rad = pos[None, :] * freqs[:, None]
        cos = np.cos(rad).astype(np.float32)
        sin = np.sin(rad).astype(np.float32)
        c1 = np.concatenate([cos, cos], axis=0)
        c2 = np.concatenate([-sin, sin], axis=0)
        tabs = np.ascontiguousarray(np.stack([c1, c2]))
        pm = np.zeros((128, PMTOT), dtype=np.float32)
        for j in range(NSLOT):
            g = 8 * j + c
            pm[:, PMB[j] + 2 * g: PMB[j] + LPRE[j]] = np.float32(NEG)
        xc = np.ascontiguousarray(x[_owned_positions(c)].T)
        m = dict(shared)
        m["x_fm"] = xc
        m["rope_tabs"] = tabs
        m["padmask"] = pm
        in_maps.append(m)
    return in_maps


def _assemble(results):
    y = np.empty((T, D), dtype=np.float32)
    for c in range(NCORE):
        y[_owned_positions(c)] = results[c]["y"]
    return y.reshape(1, T, D)


def kernel(**inputs):
    in_maps = _host_inputs(inputs)
    nc = build_nc(stage=6, dbg=False)
    res = run_bass_kernel_spmd(nc, in_maps, core_ids=list(range(NCORE)))
    return _assemble(res.results)


# version 06
# speedup vs baseline: 1.0415x; 1.0415x over previous
import numpy as np
import concourse.bass as bass
import concourse.mybir as mybir
from concourse.bass_utils import run_bass_kernel_spmd

F32 = mybir.dt.float32
BF16 = mybir.dt.bfloat16
U8 = mybir.dt.uint8
AF = mybir.ActivationFunctionType
ALU = mybir.AluOpType

NCORE = 8
T, D, H, HD = 8192, 2048, 16, 128
CH = 256
NCH = T // CH
NSLOT = 4
TOK = NSLOT * CH
NM = D // 128
FF = 4096
NFO = FF // 128
LPRE = [16 * j + 14 for j in range(NSLOT)]
PMB = [0]
for _j in range(1, NSLOT):
    PMB.append(PMB[-1] + LPRE[_j - 1])
PMTOT = PMB[-1] + LPRE[-1]
EPS = 1e-5
NEG = -1e30
ASCALE = 1.0 / 128.0

ENGS = ("tensor", "scalar", "vector", "sync", "gpsimd")


def _owned_positions(c):
    return np.concatenate(
        [np.arange(CH * (8 * j + c), CH * (8 * j + c) + CH) for j in range(NSLOT)]
    )


class Builder:
    def __init__(self, stage=6, dbg=False):
        self.stage = stage
        self.dbg = dbg
        nc = self.nc = bass.Bass("TRN2", target_bir_lowering=False)
        self.q = {e: [] for e in ENGS}
        self._wait_hw = {}
        self.sem = {}
        self.cnt = {}
        for s in ("lds", "ldg", "mm", "sc", "ve", "cc", "out"):
            self.sem[s] = nc.alloc_semaphore(s)
            self.cnt[s] = 0
        self._declare_dram()
        self._declare_sbuf()

    def w(self, eng, sname, val=None):
        v = self.cnt[sname] if val is None else val
        if v is None or v <= 0:
            return
        hw = self._wait_hw.setdefault(eng, {})
        if hw.get(sname, -1) >= v:
            return
        hw[sname] = v
        h = self.sem[sname]
        self.q[eng].append(lambda e, h=h, v=v: e.wait_ge(h, v))

    def op(self, eng, f, inc=None, n=1):
        if inc is None:
            self.q[eng].append(f)
            return None
        self.cnt[inc] += n
        h = self.sem[inc]
        self.q[eng].append(lambda e, f=f, h=h, n=n: f(e).then_inc(h, n))
        return self.cnt[inc]

    def dma(self, eng, out, in_, sname):
        return self.op(
            eng, lambda e, o=out, i=in_: e.dma_start(out=o, in_=i), inc=sname, n=16
        )

    def mark(self, eng, inc):
        return self.op(eng, lambda e: e.nop(), inc=inc)

    def _declare_dram(self):
        nc = self.nc
        t = nc.dram_tensor
        self.d_x = t("x_fm", [D, TOK], F32, kind="ExternalInput").ap()
        self.d_wq = t("wq", [D, D], F32, kind="ExternalInput").ap()
        self.d_wk = t("wk", [D, D], F32, kind="ExternalInput").ap()
        self.d_wv = t("wv", [D, D], F32, kind="ExternalInput").ap()
        self.d_wo = t("wo_t", [NM, D, 128], F32, kind="ExternalInput").ap()
        self.d_w0a = t("w0a_t", [NFO, D, 128], F32, kind="ExternalInput").ap()
        self.d_w1a = t("w1a_t", [NFO, D, 128], F32, kind="ExternalInput").ap()
        self.d_w0b = t("w0b_t", [NM, FF, 128], F32, kind="ExternalInput").ap()
        self.d_w1b = t("w1b_t", [NM, FF, 128], F32, kind="ExternalInput").ap()
        self.d_b0a = t("b0a_t", [128, NFO], F32, kind="ExternalInput").ap()
        self.d_b1a = t("b1a_t", [128, NFO], F32, kind="ExternalInput").ap()
        self.d_b0b = t("b0b_t", [128, NM], F32, kind="ExternalInput").ap()
        self.d_b1b = t("b1b_t", [128, NM], F32, kind="ExternalInput").ap()
        self.d_wg = t("wg_t", [NM, 128, 2], F32, kind="ExternalInput").ap()
        self.d_tab = t("rope_tabs", [2, 128, TOK], F32, kind="ExternalInput").ap()
        self.d_pad = t("padmask", [128, PMTOT], F32, kind="ExternalInput").ap()
        self.d_out = t("y", [TOK, D], F32, kind="ExternalOutput").ap()
        self.cc_in = t("cc_in", [NSLOT, 2, H, 128 * CH], BF16)
        self.kv_all = t("kv_all", [NCORE, NSLOT, 2, H, 128 * CH], BF16)
        self.x_spill = t("x_spill", [NM, 128, TOK], F32).ap()
        import ml_dtypes as _mld
        dmask = np.zeros((2, 128, CH), dtype=np.float32)
        for a in range(2):
            for k in range(128):
                dmask[a, k, : min(CH, 128 * a + k)] = np.float32(NEG)
        self.d_dmask = nc.inline_tensor(dmask.astype(_mld.bfloat16), name="diag_masks").ap()
        self.d_ident = nc.inline_tensor(np.eye(128, dtype=np.float32),
                                        name="ident_f32").ap()
        self.dbg_out = {}
        if self.dbg:
            st = self.stage
            if st == 1:
                self.dbg_out["xln"] = t("dbg_xln", [NM, 128, TOK], F32,
                                        kind="ExternalOutput").ap()
            if st == 2:
                self.dbg_out["q"] = t("dbg_q", [H, 128, TOK], F32,
                                      kind="ExternalOutput").ap()
            if st == 3:
                self.dbg_out["kv"] = t("dbg_kv", [NCORE, NSLOT, 2, H, 128 * CH],
                                       BF16, kind="ExternalOutput").ap()
            if st == 4:
                self.dbg_out["ot"] = t("dbg_ot", [H, 128, TOK], F32,
                                       kind="ExternalOutput").ap()
                self.dbg_out["xchk"] = t("dbg_xchk", [NM, 128, TOK], F32,
                                         kind="ExternalOutput").ap()
            if st == 5:
                self.dbg_out["x1"] = t("dbg_x1", [NM, 128, TOK], F32,
                                       kind="ExternalOutput").ap()
            if st == 6:
                self.dbg_out["xln2"] = t("dbg_xln2", [NM, 128, TOK], F32,
                                         kind="ExternalOutput").ap()
                self.dbg_out["mu8"] = t("dbg_mu8", [128, TOK], F32,
                                        kind="ExternalOutput").ap()
                self.dbg_out["out0"] = t("dbg_out0", [NM, 128, TOK], F32,
                                         kind="ExternalOutput").ap()

    def _declare_sbuf(self):
        nc = self.nc
        A = nc.alloc_sbuf_tensor
        self.ar_x = A("ar_x", [128, NM * TOK], F32)
        self.ar_a = A("ar_a", [128, 16384], BF16)
        self.ar_q = A("ar_q", [128, 16384], BF16)
        self.ar_o = A("ar_o", [128, 16384], BF16)
        self.ar_w = A("ar_w", [128, 8192], BF16)
        self.ar_t = A("ar_t", [128, 2048], F32)
        self.sb_pad = A("padmask_sb", [128, PMTOT], F32)
        self.sb_b0a = A("b0a_sb", [128, NFO], F32)
        self.sb_b1a = A("b1a_sb", [128, NFO], F32)
        self.sb_b0b = A("b0b_sb", [128, NM], F32)
        self.sb_b1b = A("b1b_sb", [128, NM], F32)
        self.sb_wg = A("wg_sb", [128, 2 * NM], BF16)
        self.sb_er = A("e_ring", [128, 4 * 512], BF16)
        self.sb_kd = A("kd_ring", [128, 2 * CH], BF16)
        self.sb_vd = A("vd_ring", [128, 2 * CH], BF16)
        self.sb_kpub = A("kpub_ring", [128, 2 * TOK], BF16)
        self.sb_vpub = A("vpub_ring", [128, 2 * 512], BF16)
        self.sb_dm = A("dmask_sb", [128, 2 * CH], BF16)
        self.sb_id = A("ident_sb", [128, 128], F32)
        self.sb_ones = A("ones_col", [128, 1], F32)
        self.sb_onesb = A("ones_col_bf", [128, 1], BF16)
        self.sb_cvec = A("c_col", [128, 1], F32)
        self.sb_eps = A("eps_c", [1, 1], F32)
        self.sb_orow = A("ones_row", [1, 128], F32)
        self.sb_rb = A("rb_sb", [128, 256], F32)
        self.sb_row = A("row_sb", [1, 1280], F32)
        self.sb_r1 = self.sb_row[0:1, 1024:1280]
        self.sb_m1 = self.sb_row[0:1, 0:1024]
        self.sb_mu8 = A("mask_u8", [128, TOK], U8)
        self.ps = nc.alloc_psum_tensor("PS", [128, 4096], F32)
        self.ar_of = self.ar_o.bitcast(F32)
        self.ar_wf = self.ar_w.bitcast(F32)
        self.ar_xb = self.ar_x.bitcast(BF16)
        self.ar_tb = self.ar_t.bitcast(BF16)

    def bank(self, b, cols=512, part=128, off=0):
        return self.ps[0:part, b * 512 + off: b * 512 + off + cols]

    def x_ch(self, m):
        return self.ar_x[:, m * TOK:(m + 1) * TOK]

    def xln_ch(self, m):
        return self.ar_a[:, m * TOK:(m + 1) * TOK]

    def q_bf(self, h):
        return self.ar_q[:, h * TOK:(h + 1) * TOK]

    def o_t(self, h):
        return self.ar_o[:, h * TOK:(h + 1) * TOK]

    def kc(self, tile):
        return self.ar_a[:, tile * 128:(tile + 1) * 128]

    def vc(self, tile):
        return self.ar_a[:, 7936 + tile * 128: 7936 + (tile + 1) * 128]

    def h_ch(self, fo):
        return self.ar_xb[:, fo * TOK:(fo + 1) * TOK]

    def out0_ch(self, fo2):
        return self.ar_q[:, fo2 * TOK:(fo2 + 1) * TOK]

    def emit_all(self):
        self.phase0_loads()
        self.v_xln = self.emit_layernorm(self.ar_of, "lds", self.v_x_loaded)
        if self.stage >= 2:
            self.phase_qkv()
        if self.stage >= 3:
            self.phase_allgather()
        if self.stage >= 4:
            self.phase_attention()
        if self.stage >= 5:
            self.phase_outproj()
        if self.stage >= 6:
            self.phase_ffn()
        self.phase_debug_and_flush()

    def phase0_loads(self):
        p = self
        p.v_x_loaded = p.dma(
            "sync",
            p.ar_x[:].rearrange("pp (m t) -> pp m t", m=NM),
            p.d_x.rearrange("(m pp) t -> pp m t", pp=128),
            "lds")
        p.dma("sync", p.sb_pad[:], p.d_pad, "lds")
        p.dma("sync", p.ar_t[:].rearrange("pp (a t) -> pp a t", a=2),
              p.d_tab.rearrange("a pp t -> pp a t"), "lds")
        p.dma("sync", p.sb_dm[:].rearrange("pp (a t) -> pp a t", a=2),
              p.d_dmask.rearrange("a pp t -> pp a t"), "lds")
        p.dma("sync", p.sb_id[:], p.d_ident, "lds")
        p.dma("sync", p.sb_b0a[:], p.d_b0a, "lds")
        p.dma("sync", p.sb_b1a[:], p.d_b1a, "lds")
        p.dma("sync", p.sb_b0b[:], p.d_b0b, "lds")
        p.dma("sync", p.sb_b1b[:], p.d_b1b, "lds")
        p.dma("gpsimd", p.sb_wg[:].rearrange("pp (m e) -> pp m e", m=NM),
              p.d_wg.rearrange("m pp e -> pp m e"), "ldg")
        p.op("vector", lambda e: e.memset(p.sb_ones[:], 1.0), inc="ve")
        p.op("vector", lambda e: e.memset(p.sb_onesb[:], 1.0), inc="ve")
        p.op("vector", lambda e: e.memset(p.sb_cvec[:], 1.0 / D), inc="ve")
        p.op("vector", lambda e: e.memset(p.sb_eps[:], float(EPS)), inc="ve")
        p.v_consts = p.op("vector", lambda e: e.memset(p.sb_orow[:], 1.0),
                          inc="ve")

    def emit_layernorm(self, scr, wsem, wval):
        p, ps = self, self.ps
        mean_b = scr[:, 0:1024]
        rstd_b = scr[:, 1024:2048]
        tmp = scr[:, 2048:3072]
        tmp2 = [scr[:, 2048:3072], scr[:, 3072:4096]]
        p.w("tensor", wsem, wval)
        p.w("tensor", "ve", p.v_consts)
        vmm = None
        for m in range(NM):
            for hh in range(2):
                vmm = p.op(
                    "tensor",
                    lambda e, m=m, hh=hh: e.matmul(
                        ps[0:1, hh * 512:(hh + 1) * 512], p.sb_cvec[:],
                        p.x_ch(m)[:, hh * 512:(hh + 1) * 512],
                        start=(m == 0), stop=(m == NM - 1)),
                    inc="mm" if (m == NM - 1 and hh == 1) else None)
        p.w("scalar", wsem, wval)
        p.w("scalar", "ve", p.v_consts)
        p.w("scalar", "mm", vmm)
        v_m1 = p.op("scalar", lambda e: e.copy(p.sb_m1[:], ps[0:1, 0:1024]),
                    inc="sc")
        p.w("tensor", "sc", v_m1)
        vmb = None
        for hh in range(2):
            vmb = p.op("tensor",
                       lambda e, hh=hh: e.matmul(
                           ps[:, 1024 + hh * 512:1024 + (hh + 1) * 512],
                           p.sb_orow[:], p.sb_m1[0:1, hh * 512:(hh + 1) * 512],
                           start=True, stop=True),
                       inc="mm" if hh == 1 else None)
        p.w("scalar", "mm", vmb)
        v_mbe = p.op("scalar", lambda e: e.copy(mean_b, ps[:, 1024:2048]),
                     inc="sc")
        p.w("vector", "sc", v_mbe)
        var_mm = {}
        for m in range(NM):
            tb = tmp2[m % 2]
            if m >= 2:
                p.w("vector", "mm", var_mm[m - 2])
            vsq = p.op("vector",
                       lambda e, m=m, tb=tb: e.tensor_sub(tb, p.x_ch(m),
                                                          mean_b), inc="ve")
            vsq = p.op("vector", lambda e, tb=tb: e.tensor_mul(tb, tb, tb),
                       inc="ve")
            p.w("tensor", "ve", vsq)
            for hh in range(2):
                vmm = p.op(
                    "tensor",
                    lambda e, m=m, hh=hh, tb=tb: e.matmul(
                        ps[0:1, 2048 + hh * 512:2048 + (hh + 1) * 512],
                        p.sb_cvec[:], tb[:, hh * 512:(hh + 1) * 512],
                        start=(m == 0), stop=(m == NM - 1)),
                    inc="mm")
            var_mm[m] = vmm
        p.w("scalar", "mm", vmm)
        p.op("scalar", lambda e: e.activation(p.sb_m1[:], ps[0:1, 2048:3072],
                                              AF.Ln, bias=p.sb_eps[:]), inc="sc")
        v_r1 = p.op("scalar", lambda e: e.activation(p.sb_m1[:], p.sb_m1[:],
                                                     AF.Exp, scale=-0.5),
                    inc="sc")
        p.w("tensor", "sc", v_r1)
        vrb = None
        for hh in range(2):
            vrb = p.op("tensor",
                       lambda e, hh=hh: e.matmul(
                           ps[:, 3072 + hh * 512:3072 + (hh + 1) * 512],
                           p.sb_orow[:], p.sb_m1[0:1, hh * 512:(hh + 1) * 512],
                           start=True, stop=True),
                       inc="mm" if hh == 1 else None)
        p.w("scalar", "mm", vrb)
        v_rbe = p.op("scalar", lambda e: e.copy(rstd_b, ps[:, 3072:4096]),
                     inc="sc")
        p.w("vector", "sc", v_rbe)
        lastv = None
        for m in range(NM):
            tb = tmp2[m % 2]
            p.op("vector", lambda e, m=m, tb=tb: e.tensor_sub(
                tb, p.x_ch(m), mean_b), inc=None)
            lastv = p.op("vector",
                         lambda e, m=m, tb=tb: e.tensor_mul(
                             p.xln_ch(m), tb, rstd_b), inc="ve")
        return lastv

    def phase_qkv(self):
        p, ps = self, self.ps

        def wpair_slot(i):
            return p.ar_w[:, (i % 2) * 4096:(i % 2) * 4096 + 4096]

        def swp_slot(lh):
            return p.ar_of[:, 3072 + lh * 1024: 3072 + (lh + 1) * 1024]

        def t1_slot(lh):
            return p.ar_of[:, 5120 + lh * 1024: 5120 + (lh + 1) * 1024]

        C1 = p.ar_t[:, 0:1024]
        C2 = p.ar_t[:, 1024:2048]

        def load_wpair(pp, wait_mm):
            is_k = pp >= 8
            hpair = pp % 8
            src = (p.d_wk if is_k else p.d_wq).rearrange(
                "(m pp2) qq -> pp2 m qq", pp2=128)[
                :, :, hpair * 256:(hpair + 1) * 256]
            dst = wpair_slot(pp)[:].rearrange("pp2 (m c) -> pp2 m c", m=NM)
            if wait_mm is not None:
                p.w("gpsimd", "mm", wait_mm)
            return p.dma("gpsimd", dst, src, "ldg")

        wpair_loaded = [load_wpair(0, None), load_wpair(1, None)]
        pair_mm_done = []
        pair_ev_done = []
        k_pub_val = {}
        for pp in range(16):
            is_k = pp >= 8
            hpair = pp % 8
            g0 = (pp % 2) * 2
            p.w("tensor", "ldg", wpair_loaded[pp])
            if pp == 0:
                p.w("tensor", "ve", self.v_xln)
            if pp >= 2:
                p.w("tensor", "ve", pair_ev_done[pp - 2])
            vmm = None
            for m in range(NM):
                for lh in range(2):
                    for hh in range(2):
                        last = m == NM - 1 and lh == 1 and hh == 1
                        vmm = p.op(
                            "tensor",
                            lambda e, m=m, lh=lh, hh=hh, g0=g0, pp=pp: e.matmul(
                                ps[:, (g0 + lh) * 1024 + hh * 512:
                                   (g0 + lh) * 1024 + (hh + 1) * 512],
                                wpair_slot(pp)[:, m * 256 + lh * 128:
                                               m * 256 + (lh + 1) * 128],
                                p.xln_ch(m)[:, hh * 512:(hh + 1) * 512],
                                start=(m == 0), stop=(m == NM - 1)),
                            inc="mm" if last else None)
            pair_mm_done.append(vmm)
            if pp + 2 < 16:
                wpair_loaded.append(load_wpair(pp + 2, vmm))
            p.w("scalar", "mm", vmm)
            p.w("vector", "mm", vmm)
            if pp == 0:
                p.w("scalar", "ve", self.v_xln)
            else:
                p.w("scalar", "ve", pair_ev_done[pp - 1])
            last_ve = None
            for lh in range(2):
                head = 2 * hpair + lh
                pg = ps[:, (g0 + lh) * 1024:(g0 + lh + 1) * 1024]
                sw, t1 = swp_slot(lh), t1_slot(lh)
                p.op("scalar",
                     lambda e, pg=pg, sw=sw: e.copy(sw[0:64, :], pg[64:128, :]),
                     inc=None)
                v_sw = p.op("scalar",
                            lambda e, pg=pg, sw=sw: e.copy(sw[64:128, :],
                                                           pg[0:64, :]),
                            inc="sc")
                dst = (p.sb_kpub[:, (head % 2) * TOK:(head % 2 + 1) * TOK]
                       if is_k else p.q_bf(head))
                p.w("vector", "sc", v_sw)
                p.op("vector", lambda e, sw=sw: e.tensor_mul(sw, sw, C2),
                     inc=None)
                p.op("vector", lambda e, pg=pg, t1=t1: e.tensor_mul(t1, pg, C1),
                     inc=None)
                if is_k and head - 2 in k_pub_val:
                    p.w("vector", "ldg", k_pub_val[head - 2])
                last_ve = p.op("vector",
                               lambda e, dst=dst, t1=t1, sw=sw: e.tensor_add(
                                   dst, t1, sw), inc="ve")
                if is_k:
                    p.w("gpsimd", "ve", last_ve)
                    for j in range(NSLOT):
                        k_pub_val[head] = p.dma(
                            "gpsimd",
                            p.cc_in[j, 0, head].rearrange("(dd t) -> dd t",
                                                          dd=128),
                            dst[:, j * CH:(j + 1) * CH], "ldg")
            pair_ev_done.append(last_ve)
        self.v_rope_done = pair_ev_done[-1]

        def wv_slot(i):
            return p.ar_o[:, (i % 2) * 8192:(i % 2) * 8192 + 8192]

        def load_wv(dg, wait_mm):
            src = p.d_wv.rearrange("(m pp2) qq -> pp2 m qq", pp2=128)[
                :, :, dg * 512:(dg + 1) * 512]
            dst = wv_slot(dg)[:].rearrange("pp2 (m c) -> pp2 m c", m=NM)
            if dg == 0:
                p.w("gpsimd", "ve", self.v_rope_done)
            if wait_mm is not None:
                p.w("gpsimd", "mm", wait_mm)
            return p.dma("gpsimd", dst, src, "ldg")

        wv_loaded = [load_wv(0, None), load_wv(1, None)]
        vev = {}
        vpub_val = {}
        wv_group_done = []
        for dg in range(4):
            p.w("tensor", "ldg", wv_loaded[dg])
            if dg == 0:
                p.w("tensor", "ve", self.v_rope_done)
            vmm = None
            for qt in range(8):
                b = qt % 4
                if b in vev:
                    p.w("tensor", "sc", vev[b])
                for m in range(NM):
                    vmm = p.op(
                        "tensor",
                        lambda e, m=m, qt=qt, dg=dg, b=b: e.matmul(
                            p.bank(b),
                            p.xln_ch(m)[:, qt * 128:(qt + 1) * 128],
                            wv_slot(dg)[:, m * 512:(m + 1) * 512],
                            start=(m == 0), stop=(m == NM - 1)),
                        inc="mm" if m == NM - 1 else None)
                p.w("scalar", "mm", vmm)
                sl = qt % 2
                if sl in vpub_val:
                    p.w("scalar", "ldg", vpub_val[sl])
                vslot = p.sb_vpub[:, sl * 512: sl * 512 + 512]
                vev[b] = p.op("scalar",
                              lambda e, vslot=vslot, b=b: e.copy(vslot,
                                                                 p.bank(b)),
                              inc="sc")
                jj, half = qt // 2, qt % 2
                dstv = p.cc_in[jj, 1, 4 * dg:4 * dg + 4].rearrange(
                    "hz (th tl dd) -> tl hz th dd", th=2, dd=128)[:, :, half]
                p.w("gpsimd", "sc", vev[b])
                vpub_val[sl] = p.dma(
                    "gpsimd", dstv,
                    vslot.rearrange("pp2 (hz dd) -> pp2 hz dd", hz=4), "ldg")
            wv_group_done.append(vmm)
            if dg + 2 < 4:
                wv_loaded.append(load_wv(dg + 2, wv_group_done[dg]))
        self.v_wv_mm_last = wv_group_done[-1]

    def phase_allgather(self):
        p = self
        p.w("gpsimd", "ldg")
        self.v_cc = p.op(
            "gpsimd",
            lambda e: e.collective_compute(
                "AllGather", ALU.bypass,
                replica_groups=[list(range(NCORE))],
                ins=[p.cc_in.ap().opt()], outs=[p.kv_all.ap().opt()]),
            inc="cc")

    def phase_attention(self):
        p, ps = self, self.ps
        p.w("sync", "cc", self.v_cc)

        release = {}
        load_done = {}

        def emit_chunk_load(h, cchunk):
            r, jj = cchunk % NCORE, cchunk // NCORE
            if (h - 1, cchunk) in release:
                p.w("sync", "mm", release[(h - 1, cchunk)])
            p.dma("sync", p.ar_a[:, 2 * cchunk * 128: 2 * cchunk * 128 + 256],
                  p.kv_all[r, jj, 0, h].rearrange("(dd t) -> dd t", dd=128),
                  "lds")
            lv = p.dma(
                "sync",
                p.ar_a[:, 7936 + 2 * cchunk * 128:
                       7936 + 2 * cchunk * 128 + 256].rearrange(
                    "pp2 (th dd) -> pp2 th dd", th=2),
                p.kv_all[r, jj, 1, h].rearrange("(th tl dd) -> tl th dd",
                                                th=2, dd=128),
                "lds")
            load_done[(h, cchunk)] = lv

        def emit_diag_load(h, j):
            par = j % 2
            key = (h, "d", par) if (h, "d", par) in release else (
                (h - 1, "d", par) if (h - 1, "d", par) in release else None)
            if key is not None:
                p.w("sync", "mm", release[key])
            p.dma("sync", p.sb_kd[:, par * CH:(par + 1) * CH],
                  p.cc_in[j, 0, h].rearrange("(dd t) -> dd t", dd=128), "lds")
            lv = p.dma(
                "sync",
                p.sb_vd[:, par * CH:(par + 1) * CH].rearrange(
                    "pp2 (th dd) -> pp2 th dd", th=2),
                p.cc_in[j, 1, h].rearrange("(th tl dd) -> tl th dd",
                                           th=2, dd=128),
                "lds")
            load_done[(h, "d", j)] = lv

        er_owner = [None, None, None, None]
        st_owner = [None, None, None]
        pv_owner = [None, None]
        on_owner = [None, None]
        rb_owner = None
        last_vnorm = None
        gidx = 0
        pending_tail = []

        for h in range(H):
            for cchunk in range(NCH - 2, -1, -1):
                emit_chunk_load(h, cchunk)
            for j in range(NSLOT - 1, -1, -1):
                par = j % 2
                emit_diag_load(h, j)
                qap = p.q_bf(h)[:, j * 256:(j + 1) * 256]
                groups = [("pre", i) for i in range(LPRE[j] // 2 - 1, -1, -1)]
                groups.append(("diag", j))
                pv_bank, on_bank = 4 + par, 6 + par
                if pv_owner[par] is not None:
                    p.w("tensor", "ve", pv_owner[par])
                if on_owner[par] is not None:
                    p.w("tensor", "ve", on_owner[par])
                def emit_pv(pg):
                    (eslot_, v0_, v1_, first_, last_, vexp_, kind_, idx_,
                     er__) = pg
                    p.w("tensor", "sc", vexp_)
                    p.op("tensor",
                         lambda e, v0=v0_, eslot=eslot_, pv_bank=pv_bank,
                         first=first_: e.matmul(
                             p.bank(pv_bank, cols=256), v0, eslot[:, 0:256],
                             start=first, stop=False), inc=None)
                    p.op("tensor",
                         lambda e, v1=v1_, eslot=eslot_, pv_bank=pv_bank,
                         last=last_: e.matmul(
                             p.bank(pv_bank, cols=256), v1, eslot[:, 256:512],
                             start=False, stop=last), inc=None)
                    p.op("tensor",
                         lambda e, eslot=eslot_, on_bank=on_bank,
                         first=first_: e.matmul(
                             p.bank(on_bank, cols=256, part=1), p.sb_onesb[:],
                             eslot[:, 0:256], start=first, stop=False),
                         inc=None)
                    vpv = p.op("tensor",
                               lambda e, eslot=eslot_, on_bank=on_bank,
                               last=last_: e.matmul(
                                   p.bank(on_bank, cols=256, part=1),
                                   p.sb_onesb[:], eslot[:, 256:512],
                                   start=False, stop=last), inc="mm")
                    er_owner[er__] = vpv
                    if kind_ == "pre":
                        release[(h, idx_)] = vpv
                    else:
                        release[(h, "d", par)] = vpv
                    return vpv

                pending = []
                for gi, (kind, idx) in enumerate(groups):
                    sb_ = gidx % 3
                    er_ = gidx % 4
                    if st_owner[sb_] is not None:
                        p.w("tensor", "sc", st_owner[sb_])
                    if kind == "pre":
                        p.w("tensor", "lds", load_done[(h, idx)])
                        k0, k1 = p.kc(2 * idx), p.kc(2 * idx + 1)
                        v0, v1 = p.vc(2 * idx), p.vc(2 * idx + 1)
                        bias = p.sb_pad[:, PMB[j] + 2 * idx:
                                        PMB[j] + 2 * idx + 1]
                    else:
                        p.w("tensor", "lds", load_done[(h, "d", idx)])
                        k0 = p.sb_kd[:, par * CH: par * CH + 128]
                        k1 = p.sb_kd[:, par * CH + 128: par * CH + 256]
                        v0 = p.sb_vd[:, par * CH: par * CH + 128]
                        v1 = p.sb_vd[:, par * CH + 128: par * CH + 256]
                        bias = None
                    p.op("tensor",
                         lambda e, k0=k0, sb_=sb_, qap=qap: e.matmul(
                             ps[:, sb_ * 512: sb_ * 512 + 256], k0, qap,
                             start=True, stop=True), inc=None)
                    vst = p.op("tensor",
                               lambda e, k1=k1, sb_=sb_, qap=qap: e.matmul(
                                   ps[:, sb_ * 512 + 256: sb_ * 512 + 512],
                                   k1, qap, start=True, stop=True), inc="mm")
                    if kind == "diag":
                        p.w("vector", "mm", vst)
                        p.op("vector",
                             lambda e, sb_=sb_: e.tensor_add(
                                 ps[:, sb_ * 512: sb_ * 512 + 256],
                                 ps[:, sb_ * 512: sb_ * 512 + 256],
                                 p.sb_dm[:, 0:256]), inc=None)
                        vdm = p.op("vector",
                                   lambda e, sb_=sb_: e.tensor_add(
                                       ps[:, sb_ * 512 + 256: sb_ * 512 + 512],
                                       ps[:, sb_ * 512 + 256: sb_ * 512 + 512],
                                       p.sb_dm[:, 256:512]), inc="ve")
                        p.w("scalar", "ve", vdm)
                    p.w("scalar", "mm", vst)
                    if er_owner[er_] is not None:
                        p.w("scalar", "mm", er_owner[er_])
                    eslot = p.sb_er[:, er_ * 512:(er_ + 1) * 512]
                    kw = {} if bias is None else {"bias": bias}
                    vexp = p.op("scalar",
                                lambda e, eslot=eslot, sb_=sb_, kw=kw:
                                e.activation(eslot,
                                             ps[:, sb_ * 512:(sb_ + 1) * 512],
                                             AF.Exp, scale=float(ASCALE), **kw),
                                inc="sc")
                    st_owner[sb_] = vexp
                    if pending_tail:
                        pending_tail.pop(0)()
                    first, last = gi == 0, gi == len(groups) - 1
                    pending.append((eslot, v0, v1, first, last, vexp, kind,
                                    idx, er_))
                    if len(pending) > 3:
                        emit_pv(pending.pop(0))
                    gidx += 1
                while pending:
                    emit_pv(pending.pop(0))
                v_ones_done = p.cnt["mm"]

                def make_tail(h=h, j=j, par=par, pv_bank=pv_bank,
                              on_bank=on_bank, v_ones_done=v_ones_done):
                    def tail():
                        nonlocal rb_owner, last_vnorm
                        p.w("vector", "mm", v_ones_done)
                        vrec = p.op("vector",
                                    lambda e: e.reciprocal(
                                        p.sb_r1[:],
                                        p.bank(on_bank, cols=256, part=1)),
                                    inc="ve")
                        on_owner[par] = vrec
                        p.w("tensor", "ve", vrec)
                        if rb_owner is not None:
                            p.w("tensor", "sc", rb_owner)
                        vrb = p.op("tensor",
                                   lambda e: e.matmul(p.bank(3, cols=256),
                                                      p.sb_orow[:], p.sb_r1[:],
                                                      start=True, stop=True),
                                   inc="mm")
                        p.w("scalar", "mm", vrb)
                        if last_vnorm is not None:
                            p.w("scalar", "ve", last_vnorm)
                        rb_owner = p.op("scalar",
                                        lambda e: e.copy(p.sb_rb[:],
                                                         p.bank(3, cols=256)),
                                        inc="sc")
                        p.w("vector", "sc", rb_owner)
                        last_vnorm = p.op(
                            "vector",
                            lambda e: e.tensor_mul(
                                p.o_t(h)[:, j * 256:(j + 1) * 256],
                                p.bank(pv_bank, cols=256), p.sb_rb[:]),
                            inc="ve")
                        pv_owner[par] = last_vnorm
                    return tail

                pending_tail.append(make_tail())
        while pending_tail:
            pending_tail.pop(0)()
        self.v_attn = p.cnt["ve"]

    def phase_outproj(self):
        p, ps = self, self.ps

        def wo_slot(i):
            return p.ar_tb[:, (i % 2) * 2048:(i % 2) * 2048 + 2048]

        def load_wo(m, wait_mm):
            src = p.d_wo[m].rearrange("(hi pp2) c -> pp2 hi c", pp2=128)
            dst = wo_slot(m)[:].rearrange("pp2 (hi c) -> pp2 hi c", hi=NM)
            if m == 0:
                p.w("gpsimd", "ve", self.v_rope_done)
            if wait_mm is not None:
                p.w("gpsimd", "mm", wait_mm)
            return p.dma("gpsimd", dst, src, "ldg")

        wo_loaded = [load_wo(0, None), load_wo(1, None)]
        wo_used = []
        v_res = [None, None]
        for m in range(NM):
            g0 = (m % 2) * 2
            p.w("tensor", "ldg", wo_loaded[m])
            if m == 0:
                p.w("tensor", "ve", self.v_attn)
            if v_res[m % 2] is not None:
                p.w("tensor", "ve", v_res[m % 2])
            vmm = None
            for hd in range(NM):
                for hh in range(2):
                    vmm = p.op(
                        "tensor",
                        lambda e, m=m, hd=hd, hh=hh, g0=g0: e.matmul(
                            ps[:, g0 * 1024 + hh * 512:
                               g0 * 1024 + (hh + 1) * 512],
                            wo_slot(m)[:, hd * 128:(hd + 1) * 128],
                            p.o_t(hd)[:, hh * 512:(hh + 1) * 512],
                            start=(hd == 0), stop=(hd == NM - 1)),
                        inc="mm" if (hd == NM - 1 and hh == 1) else None)
            wo_used.append(vmm)
            if m + 2 < NM:
                wo_loaded.append(load_wo(m + 2, vmm))
            p.w("vector", "mm", vmm)
            v_res[m % 2] = p.op(
                "vector",
                lambda e, m=m, g0=g0: e.tensor_add(
                    p.x_ch(m), p.x_ch(m), ps[:, g0 * 1024:(g0 + 1) * 1024]),
                inc="ve")
        self.v_outproj_mm = wo_used[-1]
        self.v_x1 = p.cnt["ve"]

    def phase_ffn(self):
        p, ps = self, self.ps
        v_xln2 = self.emit_layernorm(self.ar_wf, "ve", self.v_x1)
        self.v_xln2_dbg = v_xln2
        v_ln2_sc = p.cnt["sc"]
        self._ffn_body(v_xln2, v_ln2_sc)

    def _ffn_body(self, v_xln2, v_ln2_sc):
        p, ps = self, self.ps
        import os
        _upto = os.environ.get("FFN_UPTO", "full")
        if _upto == "ln2":
            return
        p.w("tensor", "ve", v_xln2)
        vz = None
        for eidx in range(2):
            base = 2048 + eidx * 1024
            for m in range(NM):
                for hh in range(2):
                    vz = p.op(
                        "tensor",
                        lambda e, eidx=eidx, m=m, hh=hh, base=base: e.matmul(
                            ps[0:1, base + hh * 512: base + (hh + 1) * 512],
                            p.sb_wg[:, 2 * m + eidx: 2 * m + eidx + 1],
                            p.xln_ch(m)[:, hh * 512:(hh + 1) * 512],
                            start=(m == 0), stop=(m == NM - 1)),
                        inc="mm" if (m == NM - 1 and hh == 1) else None)
        p.w("scalar", "mm", vz)
        p.w("scalar", "ve", v_xln2)
        zrow = self.ar_wf
        p.op("scalar", lambda e: e.copy(zrow[0:1, 0:1024],
                                        ps[0:1, 2048:3072]), inc="sc")
        v_zev = p.op("scalar", lambda e: e.copy(zrow[0:1, 1024:2048],
                                                ps[0:1, 3072:4096]), inc="sc")
        p.w("vector", "sc", v_zev)
        v_ge = p.op("vector",
                    lambda e: e.tensor_tensor(zrow[0:1, 0:1024],
                                              zrow[0:1, 0:1024],
                                              zrow[0:1, 1024:2048],
                                              ALU.is_ge), inc="ve")
        p.w("tensor", "ve", v_ge)
        vmb = None
        for hh in range(2):
            vmb = p.op("tensor",
                       lambda e, hh=hh: e.matmul(
                           ps[:, 2048 + hh * 512:2048 + (hh + 1) * 512],
                           p.sb_orow[:], zrow[0:1, hh * 512:(hh + 1) * 512],
                           start=True, stop=True),
                       inc="mm" if hh == 1 else None)
        p.w("scalar", "mm", vmb)
        v_mu8 = p.op("scalar", lambda e: e.copy(p.sb_mu8[:], ps[:, 2048:3072]),
                     inc="sc")

        if _upto == "gate":
            return
        p.w("sync", "ve", v_xln2)
        v_spill = p.dma("sync",
                        p.x_spill.rearrange("m pp2 t -> pp2 m t"),
                        p.ar_x[:].rearrange("pp2 (m t) -> pp2 m t", m=NM),
                        "lds")

        def f1_slot(i):
            return p.ar_o[:, (i % 4) * 2048:(i % 4) * 2048 + 2048]

        def f2_slot(i):
            return p.ar_o[:, 8192 + (i % 2) * 4096:
                           8192 + (i % 2) * 4096 + 4096]

        f1_loaded, f2_loaded = [], []

        def load_f1(i, wait_mm):
            e_, fo = divmod(i, NFO)
            src = (p.d_w0a if e_ == 0 else p.d_w1a)[fo].rearrange(
                "(hi pp2) c -> pp2 hi c", pp2=128)
            dst = f1_slot(i)[:].rearrange("pp2 (hi c) -> pp2 hi c", hi=NM)
            if i == 0:
                p.w("gpsimd", "mm", self.v_outproj_mm)
            if wait_mm is not None:
                p.w("gpsimd", "mm", wait_mm)
            f1_loaded.append(p.dma("gpsimd", dst, src, "ldg"))

        def load_f2(i, wait_mm):
            e_, fo2 = divmod(i, NM)
            src = (p.d_w0b if e_ == 0 else p.d_w1b)[fo2].rearrange(
                "(hi pp2) c -> pp2 hi c", pp2=128)
            dst = f2_slot(i)[:].rearrange("pp2 (hi c) -> pp2 hi c", hi=NFO)
            if wait_mm is not None:
                p.w("gpsimd", "mm", wait_mm)
            f2_loaded.append(p.dma("gpsimd", dst, src, "ldg"))

        for i in range(4):
            load_f1(i, None)

        x_rb = lambda i: p.ar_wf[:, (i % 2) * 1024:(i % 2) * 1024 + 1024]
        x2_sl = lambda i: p.ar_t[:, (i % 2) * 1024:(i % 2) * 1024 + 1024]
        xT_sl = lambda i: p.ar_wf[:, 2048 + (i % 2) * 128:
                                  2048 + (i % 2) * 128 + 128]
        out1_sl = p.ar_w[:, 5120:6144]
        sel_sl = p.ar_w[:, 6144:7168]

        h_evac = {}
        f1_owner = {}
        f2_owner = {}
        v_out0 = {}
        v_f2mm = {}
        tr_owner = [None, None]
        xT_owner = [None, None]
        x2_owner = [None, None]
        x_rb_done = {}
        last_vsel = None
        last_vx2 = None

        for e_ in range(2):
            for fo in range(NFO):
                i = e_ * NFO + fo
                par = fo % 2
                g0 = par
                p.w("tensor", "ldg", f1_loaded[i])
                if i == 0:
                    p.w("tensor", "sc", v_ln2_sc)
                    p.w("tensor", "ve", v_xln2)
                if par in f1_owner:
                    p.w("tensor", f1_owner[par][0], f1_owner[par][1])
                vmm = None
                for fi in range(NM):
                    for hh in range(2):
                        vmm = p.op(
                            "tensor",
                            lambda e, i=i, fi=fi, hh=hh, g0=g0: e.matmul(
                                ps[:, g0 * 1024 + hh * 512:
                                   g0 * 1024 + (hh + 1) * 512],
                                f1_slot(i)[:, fi * 128:(fi + 1) * 128],
                                p.xln_ch(fi)[:, hh * 512:(hh + 1) * 512],
                                start=(fi == 0), stop=(fi == NM - 1)),
                            inc="mm" if (fi == NM - 1 and hh == 1) else None)
                if i + 4 < 2 * NFO:
                    load_f1(i + 4, vmm)
                if e_ == 0 and fo >= NFO - 2:
                    load_f2(fo - (NFO - 2), vmm)
                elif e_ == 1 and fo >= NFO - 2:
                    load_f2(NM + fo - (NFO - 2),
                            v_f2mm[(0, NM - 2 + (fo - (NFO - 2)))])
                p.w("scalar", "mm", vmm)
                if i == 0:
                    p.w("scalar", "lds", v_spill)
                if e_ == 1:
                    p.w("scalar", "mm", self.v_ffn2_e0_mm)
                bias = (p.sb_b0a if e_ == 0 else p.sb_b1a)[:, fo:fo + 1]
                func = AF.Gelu if e_ == 0 else AF.Relu
                h_evac[fo] = p.op(
                    "scalar",
                    lambda e, fo=fo, g0=g0, bias=bias, func=func: e.activation(
                        p.h_ch(fo), ps[:, g0 * 1024:(g0 + 1) * 1024], func,
                        bias=bias), inc="sc")
                f1_owner[par] = ("sc", h_evac[fo])

            for fo2 in range(NM):
                i = e_ * NM + fo2
                par = fo2 % 2
                g0 = 2 + par
                p.w("tensor", "ldg", f2_loaded[i])
                if e_ == 0 and fo2 in (0, 1):
                    p.w("tensor", "sc", v_mu8)
                if par in f2_owner:
                    p.w("tensor", f2_owner[par][0], f2_owner[par][1])
                vmm = None
                for fi2 in range(NFO):
                    if fi2 % 8 == 0:
                        p.w("tensor", "sc", h_evac[min(fi2 + 7, NFO - 1)])
                    for hh in range(2):
                        vmm = p.op(
                            "tensor",
                            lambda e, i=i, fi2=fi2, hh=hh, g0=g0: e.matmul(
                                ps[:, g0 * 1024 + hh * 512:
                                   g0 * 1024 + (hh + 1) * 512],
                                f2_slot(i)[:, fi2 * 128:(fi2 + 1) * 128],
                                p.h_ch(fi2)[:, hh * 512:(hh + 1) * 512],
                                start=(fi2 == 0), stop=(fi2 == NFO - 1)),
                            inc="mm" if (fi2 == NFO - 1 and hh == 1) else None)
                v_f2mm[(e_, fo2)] = vmm
                if e_ == 0 and fo2 + 2 < NM:
                    load_f2(fo2 + 2, vmm)
                elif e_ == 0 and fo2 >= NM - 2:
                    pass
                elif e_ == 1 and fo2 + 2 < NM:
                    load_f2(NM + fo2 + 2, vmm)
                if e_ == 0:
                    p.w("scalar", "mm", vmm)
                    v_out0[fo2] = p.op(
                        "scalar",
                        lambda e, fo2=fo2, g0=g0: e.activation(
                            p.out0_ch(fo2), ps[:, g0 * 1024:(g0 + 1) * 1024],
                            AF.Identity, bias=p.sb_b0b[:, fo2:fo2 + 1]),
                        inc="sc")
                    f2_owner[par] = ("sc", v_out0[fo2])
                else:
                    if fo2 < 2:
                        k = fo2
                        p.w("sync", "ve", v_xln2)
                        p.w("sync", "sc", v_mu8)
                        x_rb_done[k] = p.dma("sync", x_rb(k), p.x_spill[k],
                                             "lds")
                    p.w("scalar", "mm", vmm)
                    if last_vsel is not None:
                        p.w("scalar", "ve", last_vsel)
                    v_o1 = p.op(
                        "scalar",
                        lambda e, fo2=fo2, g0=g0: e.activation(
                            out1_sl, ps[:, g0 * 1024:(g0 + 1) * 1024],
                            AF.Identity, bias=p.sb_b1b[:, fo2:fo2 + 1]),
                        inc="sc")
                    p.w("vector", "sc", v_o1)
                    if fo2 == 0:
                        p.w("vector", "sc", v_mu8)
                    p.w("vector", "sc", v_out0.get(fo2))
                    last_vsel = p.op("vector",
                                     lambda e, fo2=fo2: e.select(
                                         sel_sl, p.sb_mu8[:], p.out0_ch(fo2),
                                         out1_sl), inc="ve")
                    p.w("vector", "lds", x_rb_done[fo2])
                    vx2 = p.op("vector",
                               lambda e, fo2=fo2: e.tensor_add(
                                   x2_sl(fo2), x_rb(fo2), sel_sl), inc="ve")
                    last_vx2 = vx2
                    f2_owner[par] = ("ve", vx2)
                    x2_owner[par] = vx2
                    if fo2 + 2 < NM:
                        k = fo2 + 2
                        if x2_owner[k % 2] is not None:
                            p.w("sync", "ve", x2_owner[k % 2])
                        x_rb_done[k] = p.dma("sync", x_rb(k), p.x_spill[k],
                                             "lds")
                    p.w("tensor", "ve", vx2)
                    for tt in range(8):
                        tpar = tt % 2
                        if tr_owner[tpar] is not None:
                            p.w("tensor", "sc", tr_owner[tpar])
                        vtr = p.op(
                            "tensor",
                            lambda e, fo2=fo2, tt=tt, tpar=tpar: e.transpose(
                                p.bank(tpar, cols=128),
                                x2_sl(fo2)[:, tt * 128:(tt + 1) * 128],
                                p.sb_id[:]), inc="mm")
                        p.w("scalar", "mm", vtr)
                        if xT_owner[tpar] is not None:
                            p.w("scalar", "out", xT_owner[tpar])
                        vev = p.op("scalar",
                                   lambda e, tpar=tpar: e.copy(
                                       xT_sl(tpar), p.bank(tpar, cols=128)),
                                   inc="sc")
                        tr_owner[tpar] = vev
                        p.w("sync", "sc", vev)
                        xT_owner[tpar] = p.dma(
                            "sync",
                            p.d_out[tt * 128:(tt + 1) * 128,
                                    fo2 * 128:(fo2 + 1) * 128],
                            xT_sl(tpar), "out")
            if e_ == 0:
                self.v_ffn2_e0_mm = v_f2mm[(0, NM - 1)]

    def phase_debug_and_flush(self):
        p = self
        d = self.dbg_out
        if "xln" in d:
            p.w("gpsimd", "ve", self.v_xln)
            p.dma("gpsimd", d["xln"].rearrange("m pp2 t -> pp2 m t"),
                  p.ar_a[:].rearrange("pp2 (m t) -> pp2 m t", m=NM), "ldg")
        if "q" in d:
            p.w("gpsimd", "ve", self.v_rope_done)
            p.w("gpsimd", "mm", self.v_wv_mm_last)
            for h in range(H):
                p.dma("gpsimd", d["q"][h], p.q_bf(h), "ldg")
        if "kv" in d:
            p.w("sync", "cc", self.v_cc)
            p.dma("sync", d["kv"], p.kv_all.ap(), "lds")
        if "ot" in d:
            p.w("gpsimd", "ve", self.v_attn)
            for h in range(H):
                p.dma("gpsimd", d["ot"][h], p.o_t(h), "ldg")
        if "xchk" in d:
            p.w("sync", "ve", self.v_attn)
            p.dma("sync", d["xchk"].rearrange("m pp2 t -> pp2 m t"),
                  p.ar_x[:].rearrange("pp2 (m t) -> pp2 m t", m=NM), "lds")
        if "x1" in d:
            p.w("sync", "ve", self.v_x1)
            p.dma("sync", d["x1"].rearrange("m pp2 t -> pp2 m t"),
                  p.ar_x[:].rearrange("pp2 (m t) -> pp2 m t", m=NM), "lds")
        if "xln2" in d:
            p.w("gpsimd", "ve", self.v_xln2_dbg)
            p.dma("gpsimd", d["xln2"].rearrange("m pp2 t -> pp2 m t"),
                  p.ar_a[:].rearrange("pp2 (m t) -> pp2 m t", m=NM), "ldg")
            p.w("gpsimd", "out")
            p.dma("gpsimd", d["mu8"], p.sb_mu8[:], "ldg")
            p.dma("gpsimd", d["out0"].rearrange("m pp2 t -> pp2 m t"),
                  p.ar_q[:].rearrange("pp2 (m t) -> pp2 m t", m=NM), "ldg")
        p.w("sync", "lds")
        p.w("sync", "out")
        p.w("gpsimd", "ldg")

    def finish(self):
        nc = self.nc
        qs = self.q
        with nc.Block() as block:
            @block.tensor
            def _(e):
                for f in qs["tensor"]:
                    f(e)

            @block.scalar
            def _(e):
                for f in qs["scalar"]:
                    f(e)

            @block.vector
            def _(e):
                for f in qs["vector"]:
                    f(e)

            @block.sync
            def _(e):
                for f in qs["sync"]:
                    f(e)

            @block.gpsimd
            def _(e):
                for f in qs["gpsimd"]:
                    f(e)
        return nc


def build_nc(stage=6, dbg=False):
    b = Builder(stage=stage, dbg=dbg)
    b.emit_all()
    return b.finish()



def _host_inputs(inputs):
    x = np.asarray(inputs["L0_attn_x"], dtype=np.float32)[0]
    wq = np.ascontiguousarray(
        np.asarray(inputs["L0_attn_wq"], np.float32).reshape(D, D))
    wk = np.ascontiguousarray(
        np.asarray(inputs["L0_attn_wk"], np.float32).reshape(D, D))
    wv = np.ascontiguousarray(
        np.asarray(inputs["L0_attn_wv"], np.float32).reshape(D, D))
    wo = np.asarray(inputs["L0_attn_wo"], np.float32).reshape(D, D)
    wo_t = np.ascontiguousarray(wo.reshape(D, NM, 128).transpose(1, 0, 2))
    W0a = np.asarray(inputs["L0_ffn_W0a"], np.float32)
    W1a = np.asarray(inputs["L0_ffn_W1a"], np.float32)
    W0b = np.asarray(inputs["L0_ffn_W0b"], np.float32)
    W1b = np.asarray(inputs["L0_ffn_W1b"], np.float32)
    w0a_t = np.ascontiguousarray(W0a.T.reshape(D, NFO, 128).transpose(1, 0, 2))
    w1a_t = np.ascontiguousarray(W1a.T.reshape(D, NFO, 128).transpose(1, 0, 2))
    w0b_t = np.ascontiguousarray(W0b.T.reshape(FF, NM, 128).transpose(1, 0, 2))
    w1b_t = np.ascontiguousarray(W1b.T.reshape(FF, NM, 128).transpose(1, 0, 2))
    b0a_t = np.ascontiguousarray(
        np.asarray(inputs["L0_ffn_b0a"], np.float32).reshape(NFO, 128).T)
    b1a_t = np.ascontiguousarray(
        np.asarray(inputs["L0_ffn_b1a"], np.float32).reshape(NFO, 128).T)
    b0b_t = np.ascontiguousarray(
        np.asarray(inputs["L0_ffn_b0b"], np.float32).reshape(NM, 128).T)
    b1b_t = np.ascontiguousarray(
        np.asarray(inputs["L0_ffn_b1b"], np.float32).reshape(NM, 128).T)
    Wg = np.asarray(inputs["L0_ffn_Wg"], np.float32)
    wg_t = np.ascontiguousarray(Wg.T.reshape(NM, 128, 2))

    shared = dict(wq=wq, wk=wk, wv=wv, wo_t=wo_t, w0a_t=w0a_t, w1a_t=w1a_t,
                  w0b_t=w0b_t, w1b_t=w1b_t, b0a_t=b0a_t, b1a_t=b1a_t,
                  b0b_t=b0b_t, b1b_t=b1b_t, wg_t=wg_t)

    half = HD // 2
    freqs = np.power(10000.0, -np.arange(half, dtype=np.float64) / half)
    in_maps = []
    for c in range(NCORE):
        pos = _owned_positions(c).astype(np.float64)
        rad = pos[None, :] * freqs[:, None]
        cos = np.cos(rad).astype(np.float32)
        sin = np.sin(rad).astype(np.float32)
        c1 = np.concatenate([cos, cos], axis=0)
        c2 = np.concatenate([-sin, sin], axis=0)
        tabs = np.ascontiguousarray(np.stack([c1, c2]))
        pm = np.zeros((128, PMTOT), dtype=np.float32)
        for j in range(NSLOT):
            g = 8 * j + c
            pm[:, PMB[j] + 2 * g: PMB[j] + LPRE[j]] = np.float32(NEG)
        xc = np.ascontiguousarray(x[_owned_positions(c)].T)
        m = dict(shared)
        m["x_fm"] = xc
        m["rope_tabs"] = tabs
        m["padmask"] = pm
        in_maps.append(m)
    return in_maps


def _assemble(results):
    y = np.empty((T, D), dtype=np.float32)
    for c in range(NCORE):
        y[_owned_positions(c)] = results[c]["y"]
    return y.reshape(1, T, D)


def kernel(**inputs):
    in_maps = _host_inputs(inputs)
    nc = build_nc(stage=6, dbg=False)
    res = run_bass_kernel_spmd(nc, in_maps, core_ids=list(range(NCORE)))
    return _assemble(res.results)
